# version 11
# baseline (speedup 1.0000x reference)
import contextlib
import numpy as np
import concourse.bass as bass
import concourse.mybir as mybir
from concourse.bass_utils import run_bass_kernel_spmd

F32 = mybir.dt.float32
BF16 = mybir.dt.bfloat16
AF = mybir.ActivationFunctionType
ALU = mybir.AluOpType

NCORES = 8
B = 16; S = 2048; D = 1024; H = 8; HD = 128; FF = 4096; E = 4
BPC = 2
NQR = 4
TT = 512
DCH = 8
FC = 32
NEG = -4294967295.0
ISQ = float(np.float32(1.0 / np.sqrt(128.0)))
EPS = 1e-5
INV_D = float(np.float32(1.0 / 1024.0))

ENGS = ("tensor", "vector", "scalar", "sync", "gpsimd")


class P:

    def __init__(self):
        self.cmds = {e: [] for e in ENGS}
        self.cnt = {}
        self.handles = {}

    def sem(self, name):
        self.cnt[name] = 0

    def ins(self, eng, fn, inc=None, by=1):
        if inc is not None:
            self.cnt[inc] += by
            h = self.handles

            def run(Eng, _fn=fn, _i=inc, _b=by):
                _fn(Eng).then_inc(h[_i], _b)

            self.cmds[eng].append(run)
            return self.cnt[inc]
        self.cmds[eng].append(fn)
        return None

    def wait(self, eng, sem, val):
        if val <= 0:
            return
        assert val <= self.cnt[sem], f"wait-ahead {sem}: {val} > {self.cnt[sem]} on {eng}"
        h = self.handles
        self.cmds[eng].append(lambda Eng, _s=sem, _v=val: Eng.wait_ge(h[_s], _v))

    def wait_m(self, eng, marker):
        self.wait(eng, marker[0], marker[1])

    def barrier_all(self):
        snap = dict(self.cnt)
        for e in ENGS:
            for s, v in snap.items():
                self.wait(e, s, v)


def build_program():
    nc = bass.Bass("TRN2", target_bir_lowering=False)

    x_fm = nc.dram_tensor("x_fm", [BPC, NQR, DCH, 128, TT], F32, kind="ExternalInput")
    wq_t = nc.dram_tensor("wq_t", [H, 128, DCH, 128], F32, kind="ExternalInput")
    wk_t = nc.dram_tensor("wk_t", [H, 128, DCH, 128], F32, kind="ExternalInput")
    wv_t = nc.dram_tensor("wv_t", [H, 128, DCH, 128], F32, kind="ExternalInput")
    wu_t = nc.dram_tensor("wu_t", [H, 128, DCH, 128], F32, kind="ExternalInput")
    rb_t = nc.dram_tensor("rb_t", [H, 4095], F32, kind="ExternalInput")
    w1_t = nc.dram_tensor("w1_t", [E, FC, 128, DCH * 128], F32, kind="ExternalInput")
    w2_t = nc.dram_tensor("w2_t", [E, FC, 128, DCH * 128], F32, kind="ExternalInput")
    b1_t = nc.dram_tensor("b1_t", [128, E * FC], F32, kind="ExternalInput")
    b2_t = nc.dram_tensor("b2_t", [4, D], F32, kind="ExternalInput")
    wuf_t = nc.dram_tensor("wuf_t", [128, DCH], F32, kind="ExternalInput")
    bu_t = nc.dram_tensor("bu_t", [1, 1], F32, kind="ExternalInput")

    out_t = nc.dram_tensor("out", [BPC, NQR, DCH, 128, TT], F32, kind="ExternalOutput").ap()

    xp_scr = nc.dram_tensor("xp_scr", [BPC, NQR, DCH, 128, TT], F32, kind="Internal")
    w1bf = nc.dram_tensor("w1bf", [E, FC, 128, DCH * 128], BF16, kind="Internal")
    w2bf = nc.dram_tensor("w2bf", [E, FC, 128, DCH * 128], BF16, kind="Internal")
    kscr = nc.dram_tensor("kscr", [16, 128], BF16, kind="Internal")
    cscr = nc.dram_tensor("cscr", [4 * TT], F32, kind="Internal")

    CH_B = NQR * DCH * 128 * TT
    CH_Q = DCH * 128 * TT

    def chunk_src(tens, b, qr):
        return bass.AP(tens, b * CH_B + qr * CH_Q, [[TT, 128], [128 * TT, DCH], [1, TT]])

    def chunk_dst(sb):
        return bass.AP(sb, 0, [[DCH * TT, 128], [TT, DCH], [1, TT]])

    def wsl_src(tens, h):
        return bass.AP(tens, h * 128 * DCH * 128, [[DCH * 128, 128], [128, DCH], [1, 128]])

    def wsl_dst(sb):
        return bass.AP(sb, 0, [[DCH * 128, 128], [128, DCH], [1, 128]])

    p = P()
    for s in ("T", "V", "SC", "SW", "GD", "GC"):
        p.sem(s)

    with contextlib.ExitStack() as root:
        En = root.enter_context
        for s in ("T", "V", "SC", "SW", "GD", "GC"):
            p.handles[s] = En(nc.semaphore(s))

        ones_col = En(nc.sbuf_tensor("ones_col", [128, 1], BF16))
        ones_row = En(nc.sbuf_tensor("ones_row", [1, 128], F32))
        b1_s = En(nc.sbuf_tensor("b1_s", [128, E * FC], F32))
        b2_f = En(nc.sbuf_tensor("b2_f", [4, D], F32))
        b2_b = En(nc.sbuf_tensor("b2_b", [4, D], BF16))
        wuf_f = En(nc.sbuf_tensor("wuf_f", [128, DCH], F32))
        wuf_b = En(nc.sbuf_tensor("wuf_b", [128, DCH], BF16))
        bu_s = En(nc.sbuf_tensor("bu_s", [1, 1], F32))
        rows = En(nc.sbuf_tensor("rows", [1, 4 * TT], F32))
        eps_s = En(nc.sbuf_tensor("eps_s", [1, 1], F32))
        xtm1 = En(nc.sbuf_tensor("xtm1", [128, TT], F32))

        pb = [En(nc.psum_tensor(f"pb{i}", [128, 512], F32)) for i in range(8)]

        ident_b = En(nc.sbuf_tensor("ident_b", [128, 128], BF16))
        ones_rb = En(nc.sbuf_tensor("ones_rb", [1, TT], BF16))

        ph12 = root.enter_context(contextlib.ExitStack())
        E12 = ph12.enter_context
        xn = E12(nc.sbuf_tensor("xn", [128, DCH * BPC * S], BF16))
        negm = E12(nc.sbuf_tensor("negm", [128, 4 * TT], F32))
        negm_b = E12(nc.sbuf_tensor("negm_b", [128, 4 * TT], BF16))

        p.ins("vector", lambda E_: E_.memset(ones_col[:, :], 1.0), inc="V")
        p.ins("vector", lambda E_: E_.memset(ones_row[:, :], 1.0), inc="V")
        p.ins("vector", lambda E_: E_.memset(eps_s[:, :], EPS), inc="V")
        p.ins("vector", lambda E_: E_.memset(ones_rb[:, :], 1.0), inc="V")
        p.ins("sync", lambda E_: E_.dma_start(out=b1_s[:, :], in_=b1_t.ap()), inc="SW", by=16)
        p.ins("sync", lambda E_: E_.dma_start(out=b2_f[:, :], in_=b2_t.ap()), inc="SW", by=16)
        p.ins("sync", lambda E_: E_.dma_start(out=wuf_f[:, :], in_=wuf_t.ap()), inc="SW", by=16)
        sw0 = p.ins("sync", lambda E_: E_.dma_start(out=bu_s[:, :], in_=bu_t.ap()), inc="SW", by=16)
        p.wait("vector", "SW", sw0)
        p.ins("vector", lambda E_: E_.tensor_copy(b2_b[:, :], b2_f[:, :]), inc="V")
        p.ins("vector", lambda E_: E_.tensor_copy(wuf_b[:, :], wuf_f[:, :]), inc="V")
        for v in range(4):
            c = 128 * v - 384
            p.ins("gpsimd", lambda E_, _v=v, _c=c: E_.iota(
                negm[:, _v * TT:(_v + 1) * TT], [[1, TT]], base=_c,
                channel_multiplier=-1, allow_small_or_imprecise_dtypes=True), inc="GC")
        p.ins("gpsimd", lambda E_: E_.iota(
            negm_b[:, 0:128], [[1, 128]], base=0, channel_multiplier=-1,
            allow_small_or_imprecise_dtypes=True), inc="GC")
        gc_iota = p.cnt["GC"]
        p.wait("vector", "GC", gc_iota)
        p.ins("vector", lambda E_: E_.tensor_scalar(
            ident_b[:, :], negm_b[:, 0:128], 0.0, None, ALU.is_equal), inc="V")
        for v in range(4):
            sl = (v * TT, (v + 1) * TT)
            p.ins("vector", lambda E_, _s=sl: E_.tensor_scalar(
                xtm1[:, :], negm[:, _s[0]:_s[1]], 0.0, None, ALU.is_lt), inc="V")
            p.ins("vector", lambda E_, _s=sl: E_.tensor_scalar(
                negm_b[:, _s[0]:_s[1]], xtm1[:, :], NEG, None, ALU.mult), inc="V")

        pass0 = []
        for e in range(E):
            for k8 in range(0, FC, 8):
                pass0.append(lambda E_, _e=e, _k=k8: E_.dma_start(
                    out=w1bf.ap()[_e][_k:_k + 8], in_=w1_t.ap()[_e][_k:_k + 8]))
                pass0.append(lambda E_, _e=e, _k=k8: E_.dma_start(
                    out=w2bf.ap()[_e][_k:_k + 8], in_=w2_t.ap()[_e][_k:_k + 8]))
        gd_pass0 = 0
        for fn in pass0[:4]:
            gd_pass0 = p.ins("gpsimd", fn, inc="GD", by=16)
        pass0_rest = pass0[4:]

        with contextlib.ExitStack() as ph1:
            E1 = ph1.enter_context
            xin = [E1(nc.sbuf_tensor(f"xin{i}", [128, DCH * TT], F32)) for i in range(2)]
            sqb = E1(nc.sbuf_tensor("sqb", [128, DCH * TT], BF16))
            xbf = E1(nc.sbuf_tensor("xbf", [128, DCH * TT], BF16))

            xin_free = [0, 0]
            sq_free = 0
            xbf_free = 0
            stat_free = 0
            bc_free = 0
            g_ld = [0, 0]

            chunks = [(b, qr) for b in range(BPC) for qr in range(NQR)]
            for ci in range(2):
                b, qr = chunks[ci]
                g_ld[ci % 2] = p.ins("gpsimd", lambda E_, _b=b, _q=qr, _s=ci % 2:
                                     E_.dma_start(out=chunk_dst(xin[_s]),
                                                  in_=chunk_src(x_fm, _b, _q)),
                                     inc="GD", by=16)

            for ci, (b, qr) in enumerate(chunks):
                sl = ci % 2
                p.wait("scalar", "GD", g_ld[sl])
                p.wait("scalar", "T", sq_free)
                sc_sq = p.ins("scalar", lambda E_, _s=sl: E_.activation(
                    sqb[:, :], xin[_s][:, :], AF.Square), inc="SC")
                p.wait("vector", "GD", g_ld[sl])
                p.wait("vector", "T", xbf_free)
                v_cast = p.ins("vector", lambda E_, _s=sl: E_.tensor_copy(
                    xbf[:, :], xin[_s][:, :]), inc="V")
                p.wait("tensor", "SC", sc_sq)
                p.wait("tensor", "V", max(v_cast, stat_free))
                for d in range(DCH):
                    p.ins("tensor", lambda E_, _d=d: E_.matmul(
                        pb[0][0:1, 0:TT], ones_col[:, :], xbf[:, _d * TT:(_d + 1) * TT],
                        start=(_d == 0), stop=(_d == DCH - 1)), inc="T")
                t_stats = 0
                for d in range(DCH):
                    t_stats = p.ins("tensor", lambda E_, _d=d: E_.matmul(
                        pb[1][0:1, 0:TT], ones_col[:, :], sqb[:, _d * TT:(_d + 1) * TT],
                        start=(_d == 0), stop=(_d == DCH - 1)), inc="T")
                sq_free = t_stats
                xbf_free = t_stats
                p.wait("vector", "T", t_stats)
                p.ins("vector", lambda E_: E_.tensor_scalar(
                    rows[0:1, 0:TT], pb[0][0:1, 0:TT], INV_D, None, ALU.mult), inc="V")
                v_r = p.ins("vector", lambda E_: E_.tensor_scalar(
                    rows[0:1, TT:2 * TT], pb[1][0:1, 0:TT], INV_D, None, ALU.mult), inc="V")
                stat_free = v_r
                p.ins("vector", lambda E_: E_.tensor_tensor(
                    rows[0:1, 2 * TT:3 * TT], rows[0:1, 0:TT], rows[0:1, 0:TT], ALU.mult), inc="V")
                v_var = p.ins("vector", lambda E_: E_.tensor_tensor(
                    rows[0:1, 2 * TT:3 * TT], rows[0:1, TT:2 * TT], rows[0:1, 2 * TT:3 * TT],
                    ALU.subtract), inc="V")
                p.wait("scalar", "V", v_var)
                sc_sqrt = p.ins("scalar", lambda E_: E_.activation(
                    rows[0:1, 3 * TT:4 * TT], rows[0:1, 2 * TT:3 * TT], AF.Sqrt, bias=eps_s[0:1, 0:1]), inc="SC")
                p.wait("vector", "SC", sc_sqrt)
                v_rstd = p.ins("vector", lambda E_: E_.reciprocal(
                    rows[0:1, 3 * TT:4 * TT], rows[0:1, 3 * TT:4 * TT]), inc="V")
                p.wait("tensor", "V", max(v_rstd, bc_free))
                p.ins("tensor", lambda E_: E_.matmul(
                    pb[2][:, :], ones_row[:, :], rows[0:1, 0:TT], start=True, stop=True), inc="T")
                t_bc = p.ins("tensor", lambda E_: E_.matmul(
                    pb[3][:, :], ones_row[:, :], rows[0:1, 3 * TT:4 * TT], start=True, stop=True), inc="T")
                p.wait("vector", "T", t_bc)
                base = b * S + qr * TT
                v_xn = 0
                for d in range(DCH):
                    col = d * (BPC * S) + base
                    p.ins("vector", lambda E_, _s=sl, _d=d: E_.tensor_tensor(
                        xtm1[:, :], xin[_s][:, _d * TT:(_d + 1) * TT], pb[2][:, :],
                        ALU.subtract), inc="V")
                    v_xn = p.ins("vector", lambda E_, _c=col: E_.tensor_tensor(
                        xn[:, _c:_c + TT], xtm1[:, :], pb[3][:, :], ALU.mult), inc="V")
                bc_free = v_xn
                xin_free[sl] = v_xn
                if ci + 2 < len(chunks):
                    nb, nq = chunks[ci + 2]
                    p.wait("gpsimd", "V", xin_free[sl])
                    g_ld[sl] = p.ins("gpsimd", lambda E_, _b=nb, _q=nq, _s=sl:
                                     E_.dma_start(out=chunk_dst(xin[_s]),
                                                  in_=chunk_src(x_fm, _b, _q)),
                                     inc="GD", by=16)
            p.barrier_all()

        with contextlib.ExitStack() as ph2:
            E2 = ph2.enter_context
            strip = [E2(nc.sbuf_tensor(f"strip{i}", [128, 2432], F32)) for i in range(2)]
            stripb = [E2(nc.sbuf_tensor(f"stripb{i}", [128, 2432], BF16)) for i in range(4)]
            Bm = E2(nc.sbuf_tensor("Bm", [128, 4 * TT], BF16))
            wb = {w: [E2(nc.sbuf_tensor(f"w{w}b{i}", [128, DCH * 128], BF16)) for i in range(2)]
                  for w in "qkvu"}
            Qb = E2(nc.sbuf_tensor("Qb", [128, S], BF16))
            Kb = E2(nc.sbuf_tensor("Kb", [128, S], BF16))
            Vb = E2(nc.sbuf_tensor("Vb", [128, S], BF16))
            Us = E2(nc.sbuf_tensor("Us", [128, S], F32))
            kab = E2(nc.sbuf_tensor("kab", [128, S], BF16))
            wT = [E2(nc.sbuf_tensor(f"wT{i}", [128, TT], BF16)) for i in range(4)]
            nkrb = E2(nc.sbuf_tensor("nkrb", [1, S], BF16))
            xres = [E2(nc.sbuf_tensor(f"xres{i}", [128, TT], F32)) for i in range(2)]
            xout = [E2(nc.sbuf_tensor(f"xout{i}", [128, TT], F32)) for i in range(2)]

            bm_free = 0
            att_t_end = 0
            us_free = 0
            strip_free = [0, 0]
            stripb_free = [0, 0, 0, 0]
            strip_ready = [0, 0, 0, 0]
            wsl_free = [0, 0]
            wsl_ready = [0, 0]
            sbank_free = [0, 0, 0, 0]
            wT_free = [0, 0, 0, 0]
            xres_free = [0, 0]
            xout_free = [0, 0]
            pb_free_v = [0] * 8
            evac_free = [("SC", 0), ("SC", 0)]
            v_nkt = 0
            sw_kt_prev = 0
            sw_k_prev = 0
            sw_k_prev2 = 0

            def emit_strip(h, sl):
                fsl = sl % 2
                p.wait("sync", "V", strip_free[fsl])
                last = 0
                for q in range(128):
                    off = h * 4095 + 1663 - q
                    last = p.ins("sync", lambda E_, _o=off, _q=q, _sl=fsl: E_.dma_start(
                        out=strip[_sl][_q:_q + 1, :],
                        in_=bass.AP(rb_t, _o, [[2432, 1], [1, 2432]])),
                        inc="SW", by=16)
                p.wait("vector", "SW", last)
                p.wait("vector", "T", stripb_free[sl])
                strip_free[fsl] = p.ins("vector", lambda E_, _sl=sl, _f=fsl: E_.tensor_copy(
                    stripb[_sl][:, :], strip[_f][:, :]), inc="V")
                strip_ready[sl] = strip_free[fsl]

            def emit_wslices(h, sl):
                p.wait("gpsimd", "T", wsl_free[sl])
                last = 0
                for w, src in (("q", wq_t), ("k", wk_t), ("v", wv_t), ("u", wu_t)):
                    last = p.ins("gpsimd", lambda E_, _w=w, _s=src, _h=h, _sl=sl:
                                 E_.dma_start(out=wsl_dst(wb[_w][_sl]), in_=wsl_src(_s, _h)),
                                 inc="GD", by=16)
                wsl_ready[sl] = last

            emit_strip(0, 0)
            emit_wslices(0, 0)
            emit_strip(1, 1)
            emit_wslices(1, 1)

            def emit_qr_evac(h, b, qr, t_last, sc_u):
                q0 = qr * TT
                abank = 2 + (qr % 2)
                er = qr % 2
                p.wait("sync", "V", xres_free[er])
                g_x = p.ins("sync", lambda E_, _b=b, _q=qr, _h=h, _e=er:
                            E_.dma_start(out=xres[_e][:, :], in_=x_fm.ap()[_b][_q][_h]),
                            inc="SW", by=16)
                p.wait("vector", "T", t_last)
                p.wait("vector", "SC", sc_u)
                p.wait("vector", "SW", g_x)
                p.wait("vector", "SW", xout_free[er])
                p.ins("vector", lambda E_, _ab=abank, _q0=q0: E_.tensor_tensor(
                    xtm1[:, :], pb[_ab][:, :], Us[:, _q0:_q0 + TT], ALU.mult), inc="V")
                v_out = p.ins("vector", lambda E_, _e=er: E_.tensor_tensor(
                    xout[_e][:, :], xtm1[:, :], xres[_e][:, :], ALU.add), inc="V")
                pb_free_v[abank] = v_out
                xres_free[er] = v_out
                p.wait("sync", "V", v_out)
                xout_free[er] = p.ins("sync", lambda E_, _b=b, _q=qr, _h=h, _e=er:
                                      E_.dma_start(out=xp_scr.ap()[_b][_q][_h],
                                                   in_=xout[_e][:, :]),
                                      inc="SW", by=16)

            for h in range(H):
                sl = h % 2
                ssl4 = h % 4
                if h + 2 < H:
                    emit_strip(h + 2, (h + 2) % 4)
                p.wait("vector", "V", strip_ready[ssl4])
                p.wait("vector", "T", bm_free)
                p.wait("tensor", "V", strip_ready[ssl4])
                v_bm = 0
                for v in range(4):
                    g0 = 128 * v
                    v_bm = p.ins("vector", lambda E_, _g=g0, _v=v, _sl=ssl4: E_.tensor_tensor(
                        Bm[:, _v * TT:(_v + 1) * TT], stripb[_sl][:, _g:_g + TT],
                        negm_b[:, _v * TT:(_v + 1) * TT], ALU.add), inc="V")
                p.wait("tensor", "V", v_bm)

                p.wait("tensor", "GD", wsl_ready[sl])
                for b in range(BPC):
                    xnb = b * S
                    groups = []
                    for name, wkey, dst in (("K", "k", Kb), ("V", "v", Vb),
                                            ("U", "u", Us), ("Q", "q", Qb)):
                        for g in range(4):
                            groups.append((name, wkey, dst, g))

                    proj_sc = {}

                    def emit_proj(name, wkey, dst, g, gi):
                        bank = pb[gi % 2]
                        p.wait_m("tensor", evac_free[gi % 2])
                        t_mm = 0
                        if name == "V":
                            for pt in range(4):
                                ptile = g * 4 + pt
                                for d in range(DCH):
                                    col = d * (BPC * S) + xnb + ptile * 128
                                    t_mm = p.ins("tensor",
                                                 lambda E_, _bk=bank, _pt=pt, _d=d, _c=col, _w=wkey, _sl=sl:
                                                 E_.matmul(_bk[:, _pt * 128:(_pt + 1) * 128],
                                                           xn[:, _c:_c + 128],
                                                           wb[_w][_sl][:, _d * 128:(_d + 1) * 128],
                                                           start=(_d == 0), stop=(_d == DCH - 1)),
                                                 inc="T")
                        else:
                            for d in range(DCH):
                                col = d * (BPC * S) + xnb + g * TT
                                t_mm = p.ins("tensor",
                                             lambda E_, _bk=bank, _d=d, _c=col, _w=wkey, _sl=sl:
                                             E_.matmul(_bk[:, :],
                                                       wb[_w][_sl][:, _d * 128:(_d + 1) * 128],
                                                       xn[:, _c:_c + TT],
                                                       start=(_d == 0), stop=(_d == DCH - 1)),
                                             inc="T")
                        p.wait("scalar", "T", t_mm)
                        if name == "U":
                            p.wait("scalar", "V", us_free)
                        dsl = (g * TT, (g + 1) * TT)
                        sc_e = p.ins("scalar", lambda E_, _bk=bank, _d=dst, _s=dsl:
                                     E_.activation(_d[:, _s[0]:_s[1]], _bk[:, :], AF.Silu),
                                     inc="SC")
                        evac_free[gi % 2] = ("SC", sc_e)
                        proj_sc[(name, g)] = sc_e

                    for gi in range(4):
                        emit_proj(*groups[gi], gi)

                    sc_kab = p.ins("scalar", lambda E_: E_.activation(
                        kab[:, :], Kb[:, :], AF.Abs), inc="SC")
                    p.wait("tensor", "SC", sc_kab)
                    km_free = [0, 0]
                    for g in range(4):
                        bk = g % 2
                        p.wait_m("tensor", evac_free[bk])
                        p.wait("tensor", "V", km_free[bk])
                        t_kmg = p.ins("tensor", lambda E_, _g=g, _bk=bk: E_.matmul(
                            pb[_bk][0:1, 0:TT], ones_col[:, :], kab[:, _g * TT:(_g + 1) * TT],
                            start=True, stop=True), inc="T")
                        p.wait("vector", "T", t_kmg)
                        p.wait("vector", "T", att_t_end)
                        p.ins("vector", lambda E_, _g=g, _bk=bk: E_.tensor_scalar(
                            nkrb[0:1, _g * TT:(_g + 1) * TT], pb[_bk][0:1, 0:TT],
                            0.0, 1.0, ALU.is_gt, ALU.subtract), inc="V")
                        v_nk = p.ins("vector", lambda E_, _g=g: E_.tensor_scalar(
                            nkrb[0:1, _g * TT:(_g + 1) * TT], nkrb[0:1, _g * TT:(_g + 1) * TT],
                            -NEG, None, ALU.mult), inc="V")
                        km_free[bk] = v_nk
                    evac_free = [("V", km_free[0]), ("V", km_free[1])]
                    v_nkt = km_free[1]

                    for gi in (12, 13, 14, 15):
                        emit_proj(*groups[gi], gi)

                    jobs = []
                    for qr in range(NQR):
                        nkq = (qr * TT) // 128 + 4
                        for kt in range(nkq):
                            jobs.append((qr, kt, nkq))
                    inject = {0: 4, 1: 5, 2: 6, 3: 7, 4: 8, 6: 9, 8: 10, 10: 11}
                    LAG = 3
                    scw = {}
                    t_lastq = {}
                    for i in range(len(jobs) + LAG):
                        if i in inject:
                            gi = inject[i]
                            emit_proj(*groups[gi], gi)
                        if i < len(jobs):
                            qr, kt, nkq = jobs[i]
                            q0 = qr * TT
                            ssl = i % 4
                            p.wait("tensor", "SC", sbank_free[ssl])
                            p.wait("tensor", "SC", proj_sc[("Q", qr)])
                            p.wait("tensor", "SC", proj_sc[("K", kt // 4)])
                            g0 = q0 - kt * 128 + 384
                            if g0 < 512:
                                bsrc, boff = Bm, (g0 // 128) * TT
                            else:
                                bsrc, boff = stripb[ssl4], g0
                            p.ins("tensor", lambda E_, _ssl=ssl, _kt=kt, _q0=q0:
                                  E_.matmul(pb[4 + _ssl][:, :],
                                            Kb[:, _kt * 128:(_kt + 1) * 128],
                                            Qb[:, _q0:_q0 + TT],
                                            start=True, stop=False), inc="T")
                            p.wait("tensor", "V", v_nkt)
                            p.ins("tensor", lambda E_, _ssl=ssl, _kt=kt:
                                  E_.matmul(pb[4 + _ssl][:, :],
                                            nkrb[0:1, _kt * 128:(_kt + 1) * 128],
                                            ones_rb[0:1, :],
                                            start=False, stop=False), inc="T")
                            t_sc = p.ins("tensor", lambda E_, _ssl=ssl, _bs=bsrc, _bo=boff:
                                         E_.matmul(pb[4 + _ssl][:, :], ident_b[:, :],
                                                   _bs[:, _bo:_bo + TT],
                                                   start=False, stop=True), inc="T")
                            p.wait("scalar", "T", t_sc)
                            p.wait("scalar", "T", wT_free[ssl])
                            sc_w = p.ins("scalar", lambda E_, _ssl=ssl: E_.activation(
                                wT[_ssl][:, :], pb[4 + _ssl][:, :], AF.Silu,
                                scale=ISQ), inc="SC")
                            sbank_free[ssl] = sc_w
                            scw[i] = sc_w
                        if i >= LAG:
                            j = i - LAG
                            qr, kt, nkq = jobs[j]
                            ssl2 = j % 4
                            abank = 2 + (qr % 2)
                            if kt == 0:
                                p.wait("tensor", "V", pb_free_v[abank])
                            p.wait("tensor", "SC", max(scw[j], proj_sc[("V", kt // 4)]))
                            t_at = p.ins("tensor", lambda E_, _ssl=ssl2, _kt=kt, _ab=abank, _nk=nkq:
                                         E_.matmul(pb[_ab][:, :],
                                                   Vb[:, _kt * 128:(_kt + 1) * 128],
                                                   wT[_ssl][:, :],
                                                   start=(_kt == 0), stop=(_kt == _nk - 1)), inc="T")
                            wT_free[ssl2] = t_at
                            if kt == nkq - 1:
                                t_lastq[qr] = t_at
                                emit_qr_evac(h, b, qr, t_at, proj_sc[("U", qr)])
                    wsl_free[sl] = p.cnt["T"]
                    att_t_end = p.cnt["T"]
                    us_free = p.cnt["V"]

                stripb_free[ssl4] = p.cnt["T"]
                bm_free = p.cnt["T"]
                if h + 2 < H:
                    emit_wslices(h + 2, sl)
                for _ in range(4):
                    if pass0_rest:
                        gd_pass0 = p.ins("gpsimd", pass0_rest.pop(0), inc="GD", by=16)
            for fn in pass0_rest:
                gd_pass0 = p.ins("gpsimd", fn, inc="GD", by=16)
            pass0_rest = []
            p.barrier_all()
        ph12.close()

        with contextlib.ExitStack() as ph3:
            E3 = ph3.enter_context
            xpT = [E3(nc.sbuf_tensor(f"xpT{i}", [128, DCH * TT], F32)) for i in range(2)]
            sq3 = E3(nc.sbuf_tensor("sq3", [128, DCH * TT], BF16))
            xpb = E3(nc.sbuf_tensor("xpb", [128, DCH * TT], BF16))
            xn2 = E3(nc.sbuf_tensor("xn2", [128, DCH * TT], BF16))
            hsb = E3(nc.sbuf_tensor("hsb", [128, 64 * TT], BF16))
            yac = E3(nc.sbuf_tensor("yac", [128, DCH * TT], F32))
            Cb = E3(nc.sbuf_tensor("Cb", [128, 4 * TT], F32))
            Cbf = E3(nc.sbuf_tensor("Cbf", [4, TT], BF16))
            hrel = [E3(nc.sbuf_tensor(f"hrel{i}", [128, TT], F32)) for i in range(2)]
            w1b = [E3(nc.sbuf_tensor(f"w1b{i}", [128, DCH * 128], BF16)) for i in range(4)]
            w2b = [E3(nc.sbuf_tensor(f"w2b{i}", [128, DCH * 128], BF16)) for i in range(4)]
            outb = [E3(nc.sbuf_tensor(f"outb{i}", [128, TT], F32)) for i in range(2)]
            urow = E3(nc.sbuf_tensor("urow", [1, TT], F32))
            mrow = E3(nc.sbuf_tensor("mrow", [1, 4 * TT], F32))
            crow = E3(nc.sbuf_tensor("crow", [1, 4 * TT], F32))

            vsnap = p.cnt["V"]
            xpT_free = [vsnap, vsnap]
            g_xp = [0, 0]
            w1_free = [0, 0, 0, 0]
            w2_free = [0, 0, 0, 0]
            hrel_free = [0, 0]
            outb_free = [0, 0]
            pb_free = [("V", vsnap)] * 8
            hs_ready = {}
            xn2_free = 0
            cb_free = 0
            sw_c_prev = 0
            sq3_free = 0
            xpb_free = 0

            tiles = [(b, qr) for b in range(BPC) for qr in range(NQR)]
            for ci in range(2):
                b, qr = tiles[ci]
                g_xp[ci % 2] = p.ins("gpsimd", lambda E_, _b=b, _q=qr, _s=ci % 2:
                                     E_.dma_start(out=chunk_dst(xpT[_s]),
                                                  in_=chunk_src(xp_scr, _b, _q)),
                                     inc="GD", by=16)

            def w_load(which, e, fc, slot):
                buf, freel, src = (w1b, w1_free, w1bf) if which == 1 else (w2b, w2_free, w2bf)
                p.wait("sync", "T", freel[slot])
                p.wait("sync", "GD", gd_pass0)
                return p.ins("sync", lambda E_, _b=buf, _s=slot, _sr=src, _e=e, _f=fc:
                             E_.dma_start(out=_b[_s][:, :], in_=_sr.ap()[_e][_f]), inc="SW", by=16)

            for ti, (b, qr) in enumerate(tiles):
                sl = ti % 2
                p.wait("scalar", "GD", g_xp[sl])
                p.wait("scalar", "T", sq3_free)
                sc_sq = p.ins("scalar", lambda E_, _s=sl: E_.activation(
                    sq3[:, :], xpT[_s][:, :], AF.Square), inc="SC")
                p.wait("vector", "GD", g_xp[sl])
                p.wait("vector", "T", xpb_free)
                v_cast = p.ins("vector", lambda E_, _s=sl: E_.tensor_copy(
                    xpb[:, :], xpT[_s][:, :]), inc="V")
                p.wait("tensor", "SC", sc_sq)
                p.wait_m("tensor", pb_free[0])
                p.wait_m("tensor", pb_free[1])
                p.wait("tensor", "V", v_cast)
                for d in range(DCH):
                    p.ins("tensor", lambda E_, _d=d: E_.matmul(
                        pb[0][0:1, 0:TT], ones_col[:, :], xpb[:, _d * TT:(_d + 1) * TT],
                        start=(_d == 0), stop=(_d == DCH - 1)), inc="T")
                t_stats = 0
                for d in range(DCH):
                    t_stats = p.ins("tensor", lambda E_, _d=d: E_.matmul(
                        pb[1][0:1, 0:TT], ones_col[:, :], sq3[:, _d * TT:(_d + 1) * TT],
                        start=(_d == 0), stop=(_d == DCH - 1)), inc="T")
                sq3_free = t_stats
                xpb_free = t_stats
                p.wait("vector", "T", t_stats)
                p.ins("vector", lambda E_: E_.tensor_scalar(
                    rows[0:1, 0:TT], pb[0][0:1, 0:TT], INV_D, None, ALU.mult), inc="V")
                p.ins("vector", lambda E_: E_.tensor_scalar(
                    rows[0:1, TT:2 * TT], pb[1][0:1, 0:TT], INV_D, None, ALU.mult), inc="V")
                p.ins("vector", lambda E_: E_.tensor_tensor(
                    rows[0:1, 2 * TT:3 * TT], rows[0:1, 0:TT], rows[0:1, 0:TT], ALU.mult), inc="V")
                v_var = p.ins("vector", lambda E_: E_.tensor_tensor(
                    rows[0:1, 2 * TT:3 * TT], rows[0:1, TT:2 * TT], rows[0:1, 2 * TT:3 * TT],
                    ALU.subtract), inc="V")
                p.wait("scalar", "V", v_var)
                sc_sqrt = p.ins("scalar", lambda E_: E_.activation(
                    rows[0:1, 3 * TT:4 * TT], rows[0:1, 2 * TT:3 * TT], AF.Sqrt, bias=eps_s[0:1, 0:1]), inc="SC")
                p.wait("vector", "SC", sc_sqrt)
                v_rstd = p.ins("vector", lambda E_: E_.reciprocal(
                    rows[0:1, 3 * TT:4 * TT], rows[0:1, 3 * TT:4 * TT]), inc="V")
                p.wait("tensor", "V", v_rstd)
                p.ins("tensor", lambda E_: E_.matmul(
                    pb[0][:, :], ones_row[:, :], rows[0:1, 0:TT], start=True, stop=True), inc="T")
                t_bc = p.ins("tensor", lambda E_: E_.matmul(
                    pb[1][:, :], ones_row[:, :], rows[0:1, 3 * TT:4 * TT], start=True, stop=True), inc="T")
                p.wait("vector", "T", max(t_bc, xn2_free))
                v_xn2 = 0
                for d in range(DCH):
                    p.ins("vector", lambda E_, _s=sl, _d=d: E_.tensor_tensor(
                        xtm1[:, :], xpT[_s][:, _d * TT:(_d + 1) * TT], pb[0][:, :],
                        ALU.subtract), inc="V")
                    v_xn2 = p.ins("vector", lambda E_, _d=d: E_.tensor_tensor(
                        xn2[:, _d * TT:(_d + 1) * TT], xtm1[:, :], pb[1][:, :], ALU.mult), inc="V")
                pb_free[0] = ("V", v_xn2)
                pb_free[1] = ("V", v_xn2)
                p.wait("tensor", "V", v_xn2)
                p.wait_m("tensor", pb_free[2])
                t_u = 0
                for d in range(DCH):
                    t_u = p.ins("tensor", lambda E_, _d=d: E_.matmul(
                        pb[2][0:1, 0:TT], wuf_b[:, _d:_d + 1], xn2[:, _d * TT:(_d + 1) * TT],
                        start=(_d == 0), stop=(_d == DCH - 1)), inc="T")
                p.wait("scalar", "T", t_u)
                sc_u = p.ins("scalar", lambda E_: E_.activation(
                    urow[0:1, :], pb[2][0:1, 0:TT], AF.Sigmoid, bias=bu_s[0:1, 0:1]), inc="SC")
                p.wait("vector", "SC", sc_u)
                v_mr = 0
                for r in range(4):
                    p.ins("vector", lambda E_, _r=r: E_.tensor_scalar(
                        mrow[0:1, _r * TT:(_r + 1) * TT], urow[0:1, :], 4.0, float(_r),
                        ALU.mult, ALU.is_gt), inc="V")
                    p.ins("vector", lambda E_, _r=r: E_.tensor_scalar(
                        xtm1[0:1, 0:TT], urow[0:1, :], 1.0 / (_r + 1.0), None, ALU.mult), inc="V")
                    v_mr = p.ins("vector", lambda E_, _r=r: E_.tensor_tensor(
                        mrow[0:1, _r * TT:(_r + 1) * TT], mrow[0:1, _r * TT:(_r + 1) * TT],
                        xtm1[0:1, 0:TT], ALU.mult), inc="V")
                pb_free[2] = ("V", v_mr)
                p.wait("vector", "T", cb_free)
                p.wait("vector", "SW", sw_c_prev)
                v_crow = 0
                for e in range(4):
                    for r in range(4):
                        phi = (e - r) % 4
                        v_crow = p.ins("vector", lambda E_, _e=e, _r=r, _ph=phi: E_.tensor_copy(
                            bass.AP(crow, _e * TT + _ph, [[4 * TT, 1], [4, TT // 4]]),
                            bass.AP(mrow, _r * TT + _ph, [[4 * TT, 1], [4, TT // 4]])), inc="V")
                p.wait("sync", "V", v_crow)
                sw_c = p.ins("sync", lambda E_: E_.dma_start(
                    out=cscr.ap(), in_=crow[0:1, :]), inc="SW", by=16)
                sw_c_prev = sw_c
                p.wait("gpsimd", "SW", sw_c)
                p.wait("gpsimd", "T", cb_free)
                g_cbf = p.ins("gpsimd", lambda E_: E_.dma_start(
                    out=Cbf[:, :], in_=bass.AP(cscr, 0, [[TT, 4], [1, TT]])), inc="GD", by=16)
                p.wait("tensor", "V", v_crow)
                t_mb = 0
                for e in range(4):
                    p.wait_m("tensor", pb_free[e])
                    t_mb = p.ins("tensor", lambda E_, _e=e: E_.matmul(
                        pb[_e][:, :], ones_row[:, :], crow[0:1, _e * TT:(_e + 1) * TT],
                        start=True, stop=True), inc="T")
                p.wait("vector", "T", t_mb)
                v_cb = 0
                for e in range(4):
                    v_cb = p.ins("vector", lambda E_, _e=e: E_.tensor_copy(
                        Cb[:, _e * TT:(_e + 1) * TT], pb[_e][:, :]), inc="V")
                for r in range(4):
                    pb_free[r] = ("V", v_cb)

                for half in range(2):
                    es = (0, 1) if half == 0 else (2, 3)
                    for e in es:
                        for fc in range(FC):
                            k = e * FC + fc
                            wslot = k % 4
                            sw_w1 = w_load(1, e, fc, wslot)
                            bank = 4 + (k % 2)
                            p.wait("tensor", "SW", sw_w1)
                            p.wait_m("tensor", pb_free[bank])
                            t_h = 0
                            for d in range(DCH):
                                t_h = p.ins("tensor", lambda E_, _bk=bank, _w=wslot, _d=d:
                                            E_.matmul(pb[_bk][:, :],
                                                      w1b[_w][:, _d * 128:(_d + 1) * 128],
                                                      xn2[:, _d * TT:(_d + 1) * TT],
                                                      start=(_d == 0), stop=(_d == DCH - 1)),
                                            inc="T")
                            w1_free[wslot] = t_h
                            xn2_free = t_h
                            p.wait("scalar", "T", t_h)
                            p.wait("scalar", "V", hrel_free[k % 2])
                            sc_h = p.ins("scalar", lambda E_, _bk=bank, _hs=hrel[k % 2], _bc=k:
                                         E_.activation(_hs[:, :], pb[_bk][:, :], AF.Relu,
                                                       bias=b1_s[:, _bc:_bc + 1]), inc="SC")
                            pb_free[bank] = ("SC", sc_h)
                            slot = (e % 2) * FC + fc
                            p.wait("vector", "SC", sc_h)
                            v_h = p.ins("vector", lambda E_, _hs=hrel[k % 2], _sl2=slot, _e=e:
                                        E_.tensor_tensor(
                                            hsb[:, _sl2 * TT:(_sl2 + 1) * TT], _hs[:, :],
                                            Cb[:, _e * TT:(_e + 1) * TT], ALU.mult), inc="V")
                            hrel_free[k % 2] = v_h
                            hs_ready[(e, fc)] = v_h
                    t_bank_last = [0] * DCH
                    if half == 0:
                        p.wait("tensor", "GD", g_cbf)
                        for d in range(DCH):
                            p.wait_m("tensor", pb_free[d])
                            t_bank_last[d] = p.ins("tensor", lambda E_, _d=d: E_.matmul(
                                pb[_d][:, :], b2_b[:, _d * 128:(_d + 1) * 128], Cbf[:, :],
                                start=True, stop=False), inc="T")
                        cb_free = p.cnt["T"]
                    for ei, e in enumerate(es):
                        for fc in range(FC):
                            k = e * FC + fc
                            wslot = k % 4
                            sw_w2 = w_load(2, e, fc, wslot)
                            p.wait("tensor", "SW", sw_w2)
                            p.wait("tensor", "V", hs_ready[(e, fc)])
                            slot = (e % 2) * FC + fc
                            last_chunk = (ei == 1 and fc == FC - 1)
                            first_chunk = (half == 1 and ei == 0 and fc == 0)
                            for d in range(DCH):
                                if first_chunk:
                                    p.wait_m("tensor", pb_free[d])
                                t_y = p.ins("tensor",
                                            lambda E_, _d=d, _w=wslot, _sl2=slot, _st=first_chunk, _lt=last_chunk:
                                            E_.matmul(pb[_d][:, :],
                                                      w2b[_w][:, _d * 128:(_d + 1) * 128],
                                                      hsb[:, _sl2 * TT:(_sl2 + 1) * TT],
                                                      start=_st, stop=_lt), inc="T")
                                t_bank_last[d] = t_y
                            w2_free[wslot] = p.cnt["T"]
                    for d in range(DCH):
                        p.wait("vector", "T", t_bank_last[d])
                        if half == 0:
                            v_e = p.ins("vector", lambda E_, _d=d: E_.tensor_copy(
                                yac[:, _d * TT:(_d + 1) * TT], pb[_d][:, :]), inc="V")
                            pb_free[d] = ("V", v_e)
                        else:
                            p.wait("vector", "SW", outb_free[d % 2])
                            p.ins("vector", lambda E_, _d=d: E_.tensor_tensor(
                                xtm1[:, :], pb[_d][:, :], yac[:, _d * TT:(_d + 1) * TT],
                                ALU.add), inc="V")
                            v_e = p.ins("vector", lambda E_, _d=d, _ob=outb[d % 2], _s=sl:
                                        E_.tensor_tensor(
                                            _ob[:, :], xtm1[:, :], xpT[_s][:, _d * TT:(_d + 1) * TT],
                                            ALU.add), inc="V")
                            pb_free[d] = ("V", v_e)
                            p.wait("sync", "V", v_e)
                            outb_free[d % 2] = p.ins("sync",
                                                     lambda E_, _b=b, _q=qr, _d=d, _ob=outb[d % 2]:
                                                     E_.dma_start(out=out_t[_b][_q][_d], in_=_ob[:, :]),
                                                     inc="SW", by=16)
                xpT_free[sl] = p.cnt["V"]
                if ti + 2 < len(tiles):
                    nb, nq = tiles[ti + 2]
                    p.wait("gpsimd", "V", xpT_free[sl])
                    g_xp[sl] = p.ins("gpsimd", lambda E_, _b=nb, _q=nq, _s=sl:
                                     E_.dma_start(out=chunk_dst(xpT[_s]),
                                                  in_=chunk_src(xp_scr, _b, _q)),
                                     inc="GD", by=16)

            p.wait("sync", "SW", p.cnt["SW"])
            p.wait("gpsimd", "GD", p.cnt["GD"])

            with nc.Block() as block:
                for ename, dec in (("sync", block.sync), ("gpsimd", block.gpsimd),
                                   ("vector", block.vector), ("scalar", block.scalar),
                                   ("tensor", block.tensor)):
                    cmds = p.cmds[ename]
                    if not cmds:
                        continue

                    def body(Eng, _cmds=cmds):
                        for cfn in _cmds:
                            cfn(Eng)

                    dec(body)

    return nc


_CACHED = None


def _get_program():
    global _CACHED
    if _CACHED is None:
        _CACHED = build_program()
    return _CACHED


def kernel(L0_attn_x, L0_attn_Wq, L0_attn_Wk, L0_attn_Wv, L0_attn_Wu, L0_attn_rel_bias,
           L0_ffn_W1, L0_ffn_b1, L0_ffn_W2, L0_ffn_b2, L0_ffn_Wu, L0_ffn_bu):
    x = np.ascontiguousarray(np.asarray(L0_attn_x, dtype=np.float32))
    x_fm = np.ascontiguousarray(x.reshape(B, NQR, TT, DCH, 128).transpose(0, 1, 3, 4, 2))

    def wtile(W):
        W = np.asarray(W, np.float32)
        return np.ascontiguousarray(W.reshape(DCH, 128, H, 128).transpose(2, 1, 0, 3))

    wq = wtile(L0_attn_Wq); wk = wtile(L0_attn_Wk)
    wv = wtile(L0_attn_Wv); wu = wtile(L0_attn_Wu)
    rb = np.ascontiguousarray(np.asarray(L0_attn_rel_bias, np.float32).T)
    W1 = np.asarray(L0_ffn_W1, np.float32)
    w1 = np.ascontiguousarray(
        W1.reshape(E, DCH, 128, FC, 128).transpose(0, 3, 2, 1, 4).reshape(E, FC, 128, DCH * 128))
    W2 = np.asarray(L0_ffn_W2, np.float32)
    w2 = np.ascontiguousarray(W2.reshape(E, FC, 128, DCH * 128))
    b1 = np.ascontiguousarray(
        np.asarray(L0_ffn_b1, np.float32).reshape(E, FC, 128).transpose(2, 0, 1).reshape(128, E * FC))
    b2 = np.ascontiguousarray(np.asarray(L0_ffn_b2, np.float32))
    wuf = np.ascontiguousarray(np.asarray(L0_ffn_Wu, np.float32).reshape(DCH, 128).T)
    bu = np.asarray(L0_ffn_bu, np.float32).reshape(1, 1)

    nc = _get_program()
    in_maps = []
    for c in range(NCORES):
        in_maps.append(dict(
            x_fm=np.ascontiguousarray(x_fm[2 * c:2 * c + 2]),
            wq_t=wq, wk_t=wk, wv_t=wv, wu_t=wu, rb_t=rb,
            w1_t=w1, w2_t=w2, b1_t=b1, b2_t=b2, wuf_t=wuf, bu_t=bu))
    res = run_bass_kernel_spmd(nc, in_maps, core_ids=list(range(NCORES)))
    out = np.empty((B, S, D), np.float32)
    for c in range(NCORES):
        o = np.asarray(res.results[c]["out"])
        out[2 * c:2 * c + 2] = o.transpose(0, 1, 4, 2, 3).reshape(BPC, S, D)
    return out


if __name__ == "__main__":
    print("building...")
    _get_program()
    print("built ok")


# version 12
# speedup vs baseline: 1.0119x; 1.0119x over previous
import contextlib
import numpy as np
import concourse.bass as bass
import concourse.mybir as mybir
from concourse.bass_utils import run_bass_kernel_spmd

F32 = mybir.dt.float32
BF16 = mybir.dt.bfloat16
AF = mybir.ActivationFunctionType
ALU = mybir.AluOpType

NCORES = 8
B = 16; S = 2048; D = 1024; H = 8; HD = 128; FF = 4096; E = 4
BPC = 2
NQR = 4
TT = 512
DCH = 8
FC = 32
NEG = -4294967295.0
ISQ = float(np.float32(1.0 / np.sqrt(128.0)))
EPS = 1e-5
INV_D = float(np.float32(1.0 / 1024.0))

ENGS = ("tensor", "vector", "scalar", "sync", "gpsimd")


class P:

    def __init__(self):
        self.cmds = {e: [] for e in ENGS}
        self.cnt = {}
        self.handles = {}

    def sem(self, name):
        self.cnt[name] = 0

    def ins(self, eng, fn, inc=None, by=1):
        if inc is not None:
            self.cnt[inc] += by
            h = self.handles

            def run(Eng, _fn=fn, _i=inc, _b=by):
                _fn(Eng).then_inc(h[_i], _b)

            self.cmds[eng].append(run)
            return self.cnt[inc]
        self.cmds[eng].append(fn)
        return None

    def wait(self, eng, sem, val):
        if val <= 0:
            return
        assert val <= self.cnt[sem], f"wait-ahead {sem}: {val} > {self.cnt[sem]} on {eng}"
        h = self.handles
        self.cmds[eng].append(lambda Eng, _s=sem, _v=val: Eng.wait_ge(h[_s], _v))

    def wait_m(self, eng, marker):
        self.wait(eng, marker[0], marker[1])

    def barrier_all(self):
        snap = dict(self.cnt)
        for e in ENGS:
            for s, v in snap.items():
                self.wait(e, s, v)


def build_program():
    nc = bass.Bass("TRN2", target_bir_lowering=False)

    x_fm = nc.dram_tensor("x_fm", [BPC, NQR, DCH, 128, TT], F32, kind="ExternalInput")
    wq_t = nc.dram_tensor("wq_t", [H, 128, DCH, 128], F32, kind="ExternalInput")
    wk_t = nc.dram_tensor("wk_t", [H, 128, DCH, 128], F32, kind="ExternalInput")
    wv_t = nc.dram_tensor("wv_t", [H, 128, DCH, 128], F32, kind="ExternalInput")
    wu_t = nc.dram_tensor("wu_t", [H, 128, DCH, 128], F32, kind="ExternalInput")
    rb_t = nc.dram_tensor("rb_t", [H, 4095], F32, kind="ExternalInput")
    w1_t = nc.dram_tensor("w1_t", [E, FC, 128, DCH * 128], F32, kind="ExternalInput")
    w2_t = nc.dram_tensor("w2_t", [E, FC, 128, DCH * 128], F32, kind="ExternalInput")
    b1_t = nc.dram_tensor("b1_t", [128, E * FC], F32, kind="ExternalInput")
    b2_t = nc.dram_tensor("b2_t", [4, D], F32, kind="ExternalInput")
    wuf_t = nc.dram_tensor("wuf_t", [128, DCH], F32, kind="ExternalInput")
    bu_t = nc.dram_tensor("bu_t", [1, 1], F32, kind="ExternalInput")

    out_t = nc.dram_tensor("out", [BPC, NQR, DCH, 128, TT], F32, kind="ExternalOutput").ap()

    xp_scr = nc.dram_tensor("xp_scr", [BPC, NQR, DCH, 128, TT], F32, kind="Internal")
    w1bf = nc.dram_tensor("w1bf", [E, FC, 128, DCH * 128], BF16, kind="Internal")
    w2bf = nc.dram_tensor("w2bf", [E, FC, 128, DCH * 128], BF16, kind="Internal")
    kscr = nc.dram_tensor("kscr", [16, 128], BF16, kind="Internal")
    cscr = nc.dram_tensor("cscr", [4 * TT], F32, kind="Internal")

    CH_B = NQR * DCH * 128 * TT
    CH_Q = DCH * 128 * TT

    def chunk_src(tens, b, qr):
        return bass.AP(tens, b * CH_B + qr * CH_Q, [[TT, 128], [128 * TT, DCH], [1, TT]])

    def chunk_dst(sb):
        return bass.AP(sb, 0, [[DCH * TT, 128], [TT, DCH], [1, TT]])

    def wsl_src(tens, h):
        return bass.AP(tens, h * 128 * DCH * 128, [[DCH * 128, 128], [128, DCH], [1, 128]])

    def wsl_dst(sb):
        return bass.AP(sb, 0, [[DCH * 128, 128], [128, DCH], [1, 128]])

    p = P()
    for s in ("T", "V", "SC", "SW", "GD", "GC"):
        p.sem(s)

    with contextlib.ExitStack() as root:
        En = root.enter_context
        for s in ("T", "V", "SC", "SW", "GD", "GC"):
            p.handles[s] = En(nc.semaphore(s))

        ones_col = En(nc.sbuf_tensor("ones_col", [128, 1], BF16))
        ones_row = En(nc.sbuf_tensor("ones_row", [1, 128], F32))
        b1_s = En(nc.sbuf_tensor("b1_s", [128, E * FC], F32))
        b2_f = En(nc.sbuf_tensor("b2_f", [4, D], F32))
        b2_b = En(nc.sbuf_tensor("b2_b", [4, D], BF16))
        wuf_f = En(nc.sbuf_tensor("wuf_f", [128, DCH], F32))
        wuf_b = En(nc.sbuf_tensor("wuf_b", [128, DCH], BF16))
        bu_s = En(nc.sbuf_tensor("bu_s", [1, 1], F32))
        rows = En(nc.sbuf_tensor("rows", [1, 4 * TT], F32))
        eps_s = En(nc.sbuf_tensor("eps_s", [1, 1], F32))
        xtm1 = En(nc.sbuf_tensor("xtm1", [128, TT], F32))

        pb = [En(nc.psum_tensor(f"pb{i}", [128, 512], F32)) for i in range(8)]

        ident_b = En(nc.sbuf_tensor("ident_b", [128, 128], BF16))
        ones_rb = En(nc.sbuf_tensor("ones_rb", [1, TT], BF16))

        ph12 = root.enter_context(contextlib.ExitStack())
        E12 = ph12.enter_context
        xn = E12(nc.sbuf_tensor("xn", [128, DCH * BPC * S], BF16))
        negm_b = E12(nc.sbuf_tensor("negm_b", [128, 4 * TT], BF16))

        p.ins("vector", lambda E_: E_.memset(ones_col[:, :], 1.0), inc="V")
        p.ins("vector", lambda E_: E_.memset(ones_row[:, :], 1.0), inc="V")
        p.ins("vector", lambda E_: E_.memset(eps_s[:, :], EPS), inc="V")
        p.ins("vector", lambda E_: E_.memset(ones_rb[:, :], 1.0), inc="V")
        p.ins("sync", lambda E_: E_.dma_start(out=b1_s[:, :], in_=b1_t.ap()), inc="SW", by=16)
        p.ins("sync", lambda E_: E_.dma_start(out=b2_f[:, :], in_=b2_t.ap()), inc="SW", by=16)
        p.ins("sync", lambda E_: E_.dma_start(out=wuf_f[:, :], in_=wuf_t.ap()), inc="SW", by=16)
        sw0 = p.ins("sync", lambda E_: E_.dma_start(out=bu_s[:, :], in_=bu_t.ap()), inc="SW", by=16)
        p.wait("vector", "SW", sw0)
        p.ins("vector", lambda E_: E_.tensor_copy(b2_b[:, :], b2_f[:, :]), inc="V")
        p.ins("vector", lambda E_: E_.tensor_copy(wuf_b[:, :], wuf_f[:, :]), inc="V")
        for v in range(4):
            c = 128 * v - 384
            p.ins("gpsimd", lambda E_, _v=v, _c=c: E_.iota(
                negm_b[:, _v * TT:(_v + 1) * TT], [[1, TT]], base=_c,
                channel_multiplier=-1, allow_small_or_imprecise_dtypes=True), inc="GC")
        p.ins("gpsimd", lambda E_: E_.iota(
            xtm1[:, 0:128], [[1, 128]], base=0, channel_multiplier=-1,
            allow_small_or_imprecise_dtypes=True), inc="GC")
        gc_iota = p.cnt["GC"]
        p.wait("vector", "GC", gc_iota)
        p.ins("vector", lambda E_: E_.tensor_scalar(
            ident_b[:, :], xtm1[:, 0:128], 0.0, None, ALU.is_equal), inc="V")
        for v in range(4):
            sl = (v * TT, (v + 1) * TT)
            p.ins("vector", lambda E_, _s=sl: E_.tensor_scalar(
                xtm1[:, :], negm_b[:, _s[0]:_s[1]], 0.0, None, ALU.is_lt), inc="V")
            p.ins("vector", lambda E_, _s=sl: E_.tensor_scalar(
                negm_b[:, _s[0]:_s[1]], xtm1[:, :], NEG, None, ALU.mult), inc="V")

        pass0 = []
        for e in range(E):
            for k8 in range(0, FC, 8):
                pass0.append(lambda E_, _e=e, _k=k8: E_.dma_start(
                    out=w1bf.ap()[_e][_k:_k + 8], in_=w1_t.ap()[_e][_k:_k + 8]))
                pass0.append(lambda E_, _e=e, _k=k8: E_.dma_start(
                    out=w2bf.ap()[_e][_k:_k + 8], in_=w2_t.ap()[_e][_k:_k + 8]))
        gd_pass0 = 0
        for fn in pass0[:4]:
            gd_pass0 = p.ins("gpsimd", fn, inc="GD", by=16)
        pass0_rest = pass0[4:]

        with contextlib.ExitStack() as ph1:
            E1 = ph1.enter_context
            xin = [E1(nc.sbuf_tensor(f"xin{i}", [128, DCH * TT], F32)) for i in range(2)]
            sqb = E1(nc.sbuf_tensor("sqb", [128, DCH * TT], BF16))
            xbf = E1(nc.sbuf_tensor("xbf", [128, DCH * TT], BF16))

            xin_free = [0, 0]
            sq_free = 0
            xbf_free = 0
            stat_free = 0
            bc_free = 0
            g_ld = [0, 0]

            chunks = [(b, qr) for b in range(BPC) for qr in range(NQR)]
            for ci in range(2):
                b, qr = chunks[ci]
                g_ld[ci % 2] = p.ins("gpsimd", lambda E_, _b=b, _q=qr, _s=ci % 2:
                                     E_.dma_start(out=chunk_dst(xin[_s]),
                                                  in_=chunk_src(x_fm, _b, _q)),
                                     inc="GD", by=16)

            for ci, (b, qr) in enumerate(chunks):
                sl = ci % 2
                p.wait("scalar", "GD", g_ld[sl])
                p.wait("scalar", "T", sq_free)
                sc_sq = p.ins("scalar", lambda E_, _s=sl: E_.activation(
                    sqb[:, :], xin[_s][:, :], AF.Square), inc="SC")
                p.wait("vector", "GD", g_ld[sl])
                p.wait("vector", "T", xbf_free)
                v_cast = p.ins("vector", lambda E_, _s=sl: E_.tensor_copy(
                    xbf[:, :], xin[_s][:, :]), inc="V")
                p.wait("tensor", "SC", sc_sq)
                p.wait("tensor", "V", max(v_cast, stat_free))
                for d in range(DCH):
                    p.ins("tensor", lambda E_, _d=d: E_.matmul(
                        pb[0][0:1, 0:TT], ones_col[:, :], xbf[:, _d * TT:(_d + 1) * TT],
                        start=(_d == 0), stop=(_d == DCH - 1)), inc="T")
                t_stats = 0
                for d in range(DCH):
                    t_stats = p.ins("tensor", lambda E_, _d=d: E_.matmul(
                        pb[1][0:1, 0:TT], ones_col[:, :], sqb[:, _d * TT:(_d + 1) * TT],
                        start=(_d == 0), stop=(_d == DCH - 1)), inc="T")
                sq_free = t_stats
                xbf_free = t_stats
                p.wait("vector", "T", t_stats)
                p.ins("vector", lambda E_: E_.tensor_scalar(
                    rows[0:1, 0:TT], pb[0][0:1, 0:TT], INV_D, None, ALU.mult), inc="V")
                v_r = p.ins("vector", lambda E_: E_.tensor_scalar(
                    rows[0:1, TT:2 * TT], pb[1][0:1, 0:TT], INV_D, None, ALU.mult), inc="V")
                stat_free = v_r
                p.ins("vector", lambda E_: E_.tensor_tensor(
                    rows[0:1, 2 * TT:3 * TT], rows[0:1, 0:TT], rows[0:1, 0:TT], ALU.mult), inc="V")
                v_var = p.ins("vector", lambda E_: E_.tensor_tensor(
                    rows[0:1, 2 * TT:3 * TT], rows[0:1, TT:2 * TT], rows[0:1, 2 * TT:3 * TT],
                    ALU.subtract), inc="V")
                p.wait("scalar", "V", v_var)
                sc_sqrt = p.ins("scalar", lambda E_: E_.activation(
                    rows[0:1, 3 * TT:4 * TT], rows[0:1, 2 * TT:3 * TT], AF.Sqrt, bias=eps_s[0:1, 0:1]), inc="SC")
                p.wait("vector", "SC", sc_sqrt)
                v_rstd = p.ins("vector", lambda E_: E_.reciprocal(
                    rows[0:1, 3 * TT:4 * TT], rows[0:1, 3 * TT:4 * TT]), inc="V")
                p.wait("tensor", "V", max(v_rstd, bc_free))
                p.ins("tensor", lambda E_: E_.matmul(
                    pb[2][:, :], ones_row[:, :], rows[0:1, 0:TT], start=True, stop=True), inc="T")
                t_bc = p.ins("tensor", lambda E_: E_.matmul(
                    pb[3][:, :], ones_row[:, :], rows[0:1, 3 * TT:4 * TT], start=True, stop=True), inc="T")
                p.wait("vector", "T", t_bc)
                base = b * S + qr * TT
                v_xn = 0
                for d in range(DCH):
                    col = d * (BPC * S) + base
                    p.ins("vector", lambda E_, _s=sl, _d=d: E_.tensor_tensor(
                        xtm1[:, :], xin[_s][:, _d * TT:(_d + 1) * TT], pb[2][:, :],
                        ALU.subtract), inc="V")
                    v_xn = p.ins("vector", lambda E_, _c=col: E_.tensor_tensor(
                        xn[:, _c:_c + TT], xtm1[:, :], pb[3][:, :], ALU.mult), inc="V")
                bc_free = v_xn
                xin_free[sl] = v_xn
                if ci + 2 < len(chunks):
                    nb, nq = chunks[ci + 2]
                    p.wait("gpsimd", "V", xin_free[sl])
                    g_ld[sl] = p.ins("gpsimd", lambda E_, _b=nb, _q=nq, _s=sl:
                                     E_.dma_start(out=chunk_dst(xin[_s]),
                                                  in_=chunk_src(x_fm, _b, _q)),
                                     inc="GD", by=16)
            p.barrier_all()

        with contextlib.ExitStack() as ph2:
            E2 = ph2.enter_context
            strip = [E2(nc.sbuf_tensor(f"strip{i}", [128, 2432], F32)) for i in range(1)]
            stripb = [E2(nc.sbuf_tensor(f"stripb{i}", [128, 2432], BF16)) for i in range(4)]
            Bm2 = [E2(nc.sbuf_tensor(f"Bm{i}", [128, 4 * TT], BF16)) for i in range(2)]
            wb = {w: [E2(nc.sbuf_tensor(f"w{w}b{i}", [128, DCH * 128], BF16)) for i in range(2)]
                  for w in "qkvu"}
            Qb2 = [E2(nc.sbuf_tensor(f"Qb{i}", [128, S], BF16)) for i in range(2)]
            Kb2 = [E2(nc.sbuf_tensor(f"Kb{i}", [128, S], BF16)) for i in range(2)]
            Vb2 = [E2(nc.sbuf_tensor(f"Vb{i}", [128, S], BF16)) for i in range(2)]
            Us2 = [E2(nc.sbuf_tensor(f"Us{i}", [128, S], F32)) for i in range(2)]
            kab = E2(nc.sbuf_tensor("kab", [128, TT], BF16))
            wT = [E2(nc.sbuf_tensor(f"wT{i}", [128, TT], BF16)) for i in range(4)]
            nkrb2 = [E2(nc.sbuf_tensor(f"nkrb{i}", [1, S], BF16)) for i in range(2)]
            xres = [E2(nc.sbuf_tensor(f"xres{i}", [128, TT], F32)) for i in range(2)]
            xout = [E2(nc.sbuf_tensor(f"xout{i}", [128, TT], F32)) for i in range(2)]

            units = [(h, b) for h in range(H) for b in range(BPC)]
            NU = len(units)

            bm_free = [0, 0]
            nkrb_free = [0, 0]
            us_free = [0, 0]
            strip_free = [0]
            stripb_free = [0, 0, 0, 0]
            strip_ready = [0, 0, 0, 0]
            wsl_free = [0, 0]
            wsl_ready = [0, 0]
            sbank_free = [0, 0, 0, 0]
            wT_free = [0, 0, 0, 0]
            xres_free = [0, 0]
            xout_free = [0, 0]
            pb_free_v = [0] * 8
            evac_free = [("SC", 0), ("SC", 0)]
            psc = {}
            vnk = {}
            vbm = {}

            def emit_strip(h, sl):
                fsl = 0
                p.wait("sync", "V", strip_free[fsl])
                last = 0
                for q in range(128):
                    off = h * 4095 + 1663 - q
                    last = p.ins("sync", lambda E_, _o=off, _q=q, _sl=fsl: E_.dma_start(
                        out=strip[_sl][_q:_q + 1, :],
                        in_=bass.AP(rb_t, _o, [[2432, 1], [1, 2432]])),
                        inc="SW", by=16)
                p.wait("vector", "SW", last)
                p.wait("vector", "T", stripb_free[sl])
                strip_free[fsl] = p.ins("vector", lambda E_, _sl=sl, _f=fsl: E_.tensor_copy(
                    stripb[_sl][:, :], strip[_f][:, :]), inc="V")
                strip_ready[sl] = strip_free[fsl]

            def emit_wslices(h, sl):
                p.wait("gpsimd", "T", wsl_free[sl])
                last = 0
                for w, src in (("q", wq_t), ("k", wk_t), ("v", wv_t), ("u", wu_t)):
                    last = p.ins("gpsimd", lambda E_, _w=w, _s=src, _h=h, _sl=sl:
                                 E_.dma_start(out=wsl_dst(wb[_w][_sl]), in_=wsl_src(_s, _h)),
                                 inc="GD", by=16)
                wsl_ready[sl] = last

            emit_strip(0, 0)
            emit_wslices(0, 0)
            emit_strip(1, 1)
            emit_wslices(1, 1)

            def emit_qr_evac(h, b, qr, par, t_last, sc_u):
                q0 = qr * TT
                abank = 2 + (qr % 2)
                er = qr % 2
                p.wait("sync", "V", xres_free[er])
                g_x = p.ins("sync", lambda E_, _b=b, _q=qr, _h=h, _e=er:
                            E_.dma_start(out=xres[_e][:, :], in_=x_fm.ap()[_b][_q][_h]),
                            inc="SW", by=16)
                p.wait("vector", "T", t_last)
                p.wait("vector", "SC", sc_u)
                p.wait("vector", "SW", g_x)
                p.wait("vector", "SW", xout_free[er])
                p.ins("vector", lambda E_, _ab=abank, _q0=q0, _p=par: E_.tensor_tensor(
                    xtm1[:, :], pb[_ab][:, :], Us2[_p][:, _q0:_q0 + TT], ALU.mult), inc="V")
                v_out = p.ins("vector", lambda E_, _e=er: E_.tensor_tensor(
                    xout[_e][:, :], xtm1[:, :], xres[_e][:, :], ALU.add), inc="V")
                pb_free_v[abank] = v_out
                xres_free[er] = v_out
                p.wait("sync", "V", v_out)
                xout_free[er] = p.ins("sync", lambda E_, _b=b, _q=qr, _h=h, _e=er:
                                      E_.dma_start(out=xp_scr.ap()[_b][_q][_h],
                                                   in_=xout[_e][:, :]),
                                      inc="SW", by=16)

            def gen_proj(u):
                h, b = units[u]
                par = u % 2
                sl = h % 2
                ssl4 = h % 4
                xnb = b * S
                psc[u] = {}
                if b == 0:
                    if h + 2 < H:
                        emit_strip(h + 2, (h + 2) % 4)
                    p.wait("vector", "V", strip_ready[ssl4])
                    p.wait("vector", "T", bm_free[sl])
                    vb = 0
                    for v in range(4):
                        g0 = 128 * v
                        vb = p.ins("vector", lambda E_, _g=g0, _v=v, _sl=ssl4, _p=sl:
                                   E_.tensor_tensor(
                                       Bm2[_p][:, _v * TT:(_v + 1) * TT],
                                       stripb[_sl][:, _g:_g + TT],
                                       negm_b[:, _v * TT:(_v + 1) * TT], ALU.add), inc="V")
                    vbm[h] = vb
                    p.wait("tensor", "GD", wsl_ready[sl])
                yield

                def emit_proj(name, wkey, g, gi):
                    dst = {"K": Kb2, "V": Vb2, "U": Us2, "Q": Qb2}[name][par]
                    bank = pb[gi % 2]
                    p.wait_m("tensor", evac_free[gi % 2])
                    t_mm = 0
                    if name == "V":
                        for pt in range(4):
                            ptile = g * 4 + pt
                            for d in range(DCH):
                                col = d * (BPC * S) + xnb + ptile * 128
                                t_mm = p.ins("tensor",
                                             lambda E_, _bk=bank, _pt=pt, _d=d, _c=col, _w=wkey, _sl=sl:
                                             E_.matmul(_bk[:, _pt * 128:(_pt + 1) * 128],
                                                       xn[:, _c:_c + 128],
                                                       wb[_w][_sl][:, _d * 128:(_d + 1) * 128],
                                                       start=(_d == 0), stop=(_d == DCH - 1)),
                                             inc="T")
                    else:
                        for d in range(DCH):
                            col = d * (BPC * S) + xnb + g * TT
                            t_mm = p.ins("tensor",
                                         lambda E_, _bk=bank, _d=d, _c=col, _w=wkey, _sl=sl:
                                         E_.matmul(_bk[:, :],
                                                   wb[_w][_sl][:, _d * 128:(_d + 1) * 128],
                                                   xn[:, _c:_c + TT],
                                                   start=(_d == 0), stop=(_d == DCH - 1)),
                                         inc="T")
                    p.wait("scalar", "T", t_mm)
                    if name == "U":
                        p.wait("scalar", "V", us_free[par])
                    dsl = (g * TT, (g + 1) * TT)
                    sc_e = p.ins("scalar", lambda E_, _bk=bank, _d=dst, _s=dsl:
                                 E_.activation(_d[:, _s[0]:_s[1]], _bk[:, :], AF.Silu),
                                 inc="SC")
                    evac_free[gi % 2] = ("SC", sc_e)
                    psc[u][(name, g)] = sc_e

                for g in range(4):
                    emit_proj("K", "k", g, g)
                    yield
                km_free = [0, 0]
                t_kmg = 0
                for g in range(4):
                    bk = g % 2
                    p.wait("scalar", "T", t_kmg)
                    sc_ab = p.ins("scalar", lambda E_, _g=g, _p=par: E_.activation(
                        kab[:, :], Kb2[_p][:, _g * TT:(_g + 1) * TT], AF.Abs), inc="SC")
                    p.wait_m("tensor", evac_free[bk])
                    p.wait("tensor", "SC", sc_ab)
                    p.wait("tensor", "V", km_free[bk])
                    t_kmg = p.ins("tensor", lambda E_, _bk=bk: E_.matmul(
                        pb[_bk][0:1, 0:TT], ones_col[:, :], kab[:, :],
                        start=True, stop=True), inc="T")
                    p.wait("vector", "T", t_kmg)
                    p.wait("vector", "T", nkrb_free[par])
                    p.ins("vector", lambda E_, _g=g, _bk=bk, _p=par: E_.tensor_scalar(
                        nkrb2[_p][0:1, _g * TT:(_g + 1) * TT], pb[_bk][0:1, 0:TT],
                        0.0, 1.0, ALU.is_gt, ALU.subtract), inc="V")
                    v_nk = p.ins("vector", lambda E_, _g=g, _p=par: E_.tensor_scalar(
                        nkrb2[_p][0:1, _g * TT:(_g + 1) * TT],
                        nkrb2[_p][0:1, _g * TT:(_g + 1) * TT],
                        -NEG, None, ALU.mult), inc="V")
                    km_free[bk] = v_nk
                    yield
                evac_free[0] = ("V", km_free[0])
                evac_free[1] = ("V", km_free[1])
                vnk[u] = max(km_free)
                for g in range(4):
                    emit_proj("Q", "q", g, 12 + g)
                    yield
                for g in range(4):
                    emit_proj("V", "v", g, 4 + g)
                    yield
                for g in range(4):
                    emit_proj("U", "u", g, 8 + g)
                    yield
                wsl_free[sl] = p.cnt["T"]
                if b == 1:
                    if h + 2 < H:
                        emit_wslices(h + 2, sl)
                    for _ in range(4):
                        if pass0_rest:
                            globals_pass0()
                yield

            def globals_pass0():
                nonlocal gd_pass0
                gd_pass0 = p.ins("gpsimd", pass0_rest.pop(0), inc="GD", by=16)

            def gen_attn(u):
                h, b = units[u]
                par = u % 2
                ssl4 = h % 4
                jobs = []
                for qr in range(NQR):
                    nkq = (qr * TT) // 128 + 4
                    for kt in range(nkq):
                        jobs.append((qr, kt, nkq))
                LAG = 3
                scw = {}
                for i in range(len(jobs) + LAG):
                    if i < len(jobs):
                        qr, kt, nkq = jobs[i]
                        q0 = qr * TT
                        ssl = i % 4
                        p.wait("tensor", "SC", sbank_free[ssl])
                        p.wait("tensor", "SC", psc[u][("Q", qr)])
                        p.wait("tensor", "SC", psc[u][("K", kt // 4)])
                        g0 = q0 - kt * 128 + 384
                        if g0 < 512:
                            bsrc, boff = Bm2[h % 2], (g0 // 128) * TT
                            p.wait("tensor", "V", vbm[h])
                        else:
                            bsrc, boff = stripb[ssl4], g0
                        p.ins("tensor", lambda E_, _ssl=ssl, _kt=kt, _q0=q0, _p=par:
                              E_.matmul(pb[4 + _ssl][:, :],
                                        Kb2[_p][:, _kt * 128:(_kt + 1) * 128],
                                        Qb2[_p][:, _q0:_q0 + TT],
                                        start=True, stop=False), inc="T")
                        p.wait("tensor", "V", vnk[u])
                        p.ins("tensor", lambda E_, _ssl=ssl, _kt=kt, _p=par:
                              E_.matmul(pb[4 + _ssl][:, :],
                                        nkrb2[_p][0:1, _kt * 128:(_kt + 1) * 128],
                                        ones_rb[0:1, :],
                                        start=False, stop=False), inc="T")
                        t_sc = p.ins("tensor", lambda E_, _ssl=ssl, _bs=bsrc, _bo=boff:
                                     E_.matmul(pb[4 + _ssl][:, :], ident_b[:, :],
                                               _bs[:, _bo:_bo + TT],
                                               start=False, stop=True), inc="T")
                        p.wait("scalar", "T", t_sc)
                        p.wait("scalar", "T", wT_free[ssl])
                        sc_w = p.ins("scalar", lambda E_, _ssl=ssl: E_.activation(
                            wT[_ssl][:, :], pb[4 + _ssl][:, :], AF.Silu,
                            scale=ISQ), inc="SC")
                        sbank_free[ssl] = sc_w
                        scw[i] = sc_w
                    if i >= LAG:
                        j = i - LAG
                        qr, kt, nkq = jobs[j]
                        ssl2 = j % 4
                        abank = 2 + (qr % 2)
                        if kt == 0:
                            p.wait("tensor", "V", pb_free_v[abank])
                        p.wait("tensor", "SC", max(scw[j], psc[u][("V", kt // 4)]))
                        t_at = p.ins("tensor", lambda E_, _ssl=ssl2, _kt=kt, _ab=abank, _nk=nkq, _p=par:
                                     E_.matmul(pb[_ab][:, :],
                                               Vb2[_p][:, _kt * 128:(_kt + 1) * 128],
                                               wT[_ssl][:, :],
                                               start=(_kt == 0), stop=(_kt == _nk - 1)), inc="T")
                        wT_free[ssl2] = t_at
                        if kt == nkq - 1:
                            emit_qr_evac(h, b, qr, par, t_at, psc[u][("U", qr)])
                    yield
                stripb_free[ssl4] = p.cnt["T"]
                bm_free[h % 2] = p.cnt["T"]
                nkrb_free[par] = p.cnt["T"]
                us_free[par] = p.cnt["V"]
                yield

            gp = gen_proj(0)
            for _ in gp:
                pass
            for u in range(1, NU + 1):
                gp = gen_proj(u) if u < NU else None
                ga = gen_attn(u - 1)
                done_p = gp is None
                done_a = False
                while not (done_p and done_a):
                    if not done_p:
                        try:
                            next(gp)
                        except StopIteration:
                            done_p = True
                    if not done_a:
                        for _ in range(2):
                            try:
                                next(ga)
                            except StopIteration:
                                done_a = True
                                break
            p.barrier_all()
        ph12.close()

        with contextlib.ExitStack() as ph3:
            E3 = ph3.enter_context
            xpT = [E3(nc.sbuf_tensor(f"xpT{i}", [128, DCH * TT], F32)) for i in range(2)]
            sq3 = E3(nc.sbuf_tensor("sq3", [128, DCH * TT], BF16))
            xpb = E3(nc.sbuf_tensor("xpb", [128, DCH * TT], BF16))
            xn2 = E3(nc.sbuf_tensor("xn2", [128, DCH * TT], BF16))
            hsb = E3(nc.sbuf_tensor("hsb", [128, 64 * TT], BF16))
            yac = E3(nc.sbuf_tensor("yac", [128, DCH * TT], F32))
            Cb = E3(nc.sbuf_tensor("Cb", [128, 4 * TT], F32))
            Cbf = E3(nc.sbuf_tensor("Cbf", [4, TT], BF16))
            hrel = [E3(nc.sbuf_tensor(f"hrel{i}", [128, TT], F32)) for i in range(2)]
            w1b = [E3(nc.sbuf_tensor(f"w1b{i}", [128, DCH * 128], BF16)) for i in range(4)]
            w2b = [E3(nc.sbuf_tensor(f"w2b{i}", [128, DCH * 128], BF16)) for i in range(4)]
            outb = [E3(nc.sbuf_tensor(f"outb{i}", [128, TT], F32)) for i in range(2)]
            urow = E3(nc.sbuf_tensor("urow", [1, TT], F32))
            mrow = E3(nc.sbuf_tensor("mrow", [1, 4 * TT], F32))
            crow = E3(nc.sbuf_tensor("crow", [1, 4 * TT], F32))

            vsnap = p.cnt["V"]
            xpT_free = [vsnap, vsnap]
            g_xp = [0, 0]
            w1_free = [0, 0, 0, 0]
            w2_free = [0, 0, 0, 0]
            hrel_free = [0, 0]
            outb_free = [0, 0]
            pb_free = [("V", vsnap)] * 8
            hs_ready = {}
            xn2_free = 0
            cb_free = 0
            sw_c_prev = 0
            sq3_free = 0
            xpb_free = 0

            tiles = [(b, qr) for b in range(BPC) for qr in range(NQR)]
            for ci in range(2):
                b, qr = tiles[ci]
                g_xp[ci % 2] = p.ins("gpsimd", lambda E_, _b=b, _q=qr, _s=ci % 2:
                                     E_.dma_start(out=chunk_dst(xpT[_s]),
                                                  in_=chunk_src(xp_scr, _b, _q)),
                                     inc="GD", by=16)

            def w_load(which, e, fc, slot):
                buf, freel, src = (w1b, w1_free, w1bf) if which == 1 else (w2b, w2_free, w2bf)
                p.wait("sync", "T", freel[slot])
                p.wait("sync", "GD", gd_pass0)
                return p.ins("sync", lambda E_, _b=buf, _s=slot, _sr=src, _e=e, _f=fc:
                             E_.dma_start(out=_b[_s][:, :], in_=_sr.ap()[_e][_f]), inc="SW", by=16)

            for ti, (b, qr) in enumerate(tiles):
                sl = ti % 2
                p.wait("scalar", "GD", g_xp[sl])
                p.wait("scalar", "T", sq3_free)
                sc_sq = p.ins("scalar", lambda E_, _s=sl: E_.activation(
                    sq3[:, :], xpT[_s][:, :], AF.Square), inc="SC")
                p.wait("vector", "GD", g_xp[sl])
                p.wait("vector", "T", xpb_free)
                v_cast = p.ins("vector", lambda E_, _s=sl: E_.tensor_copy(
                    xpb[:, :], xpT[_s][:, :]), inc="V")
                p.wait("tensor", "SC", sc_sq)
                p.wait_m("tensor", pb_free[0])
                p.wait_m("tensor", pb_free[1])
                p.wait("tensor", "V", v_cast)
                for d in range(DCH):
                    p.ins("tensor", lambda E_, _d=d: E_.matmul(
                        pb[0][0:1, 0:TT], ones_col[:, :], xpb[:, _d * TT:(_d + 1) * TT],
                        start=(_d == 0), stop=(_d == DCH - 1)), inc="T")
                t_stats = 0
                for d in range(DCH):
                    t_stats = p.ins("tensor", lambda E_, _d=d: E_.matmul(
                        pb[1][0:1, 0:TT], ones_col[:, :], sq3[:, _d * TT:(_d + 1) * TT],
                        start=(_d == 0), stop=(_d == DCH - 1)), inc="T")
                sq3_free = t_stats
                xpb_free = t_stats
                p.wait("vector", "T", t_stats)
                p.ins("vector", lambda E_: E_.tensor_scalar(
                    rows[0:1, 0:TT], pb[0][0:1, 0:TT], INV_D, None, ALU.mult), inc="V")
                p.ins("vector", lambda E_: E_.tensor_scalar(
                    rows[0:1, TT:2 * TT], pb[1][0:1, 0:TT], INV_D, None, ALU.mult), inc="V")
                p.ins("vector", lambda E_: E_.tensor_tensor(
                    rows[0:1, 2 * TT:3 * TT], rows[0:1, 0:TT], rows[0:1, 0:TT], ALU.mult), inc="V")
                v_var = p.ins("vector", lambda E_: E_.tensor_tensor(
                    rows[0:1, 2 * TT:3 * TT], rows[0:1, TT:2 * TT], rows[0:1, 2 * TT:3 * TT],
                    ALU.subtract), inc="V")
                p.wait("scalar", "V", v_var)
                sc_sqrt = p.ins("scalar", lambda E_: E_.activation(
                    rows[0:1, 3 * TT:4 * TT], rows[0:1, 2 * TT:3 * TT], AF.Sqrt, bias=eps_s[0:1, 0:1]), inc="SC")
                p.wait("vector", "SC", sc_sqrt)
                v_rstd = p.ins("vector", lambda E_: E_.reciprocal(
                    rows[0:1, 3 * TT:4 * TT], rows[0:1, 3 * TT:4 * TT]), inc="V")
                p.wait("tensor", "V", v_rstd)
                p.ins("tensor", lambda E_: E_.matmul(
                    pb[0][:, :], ones_row[:, :], rows[0:1, 0:TT], start=True, stop=True), inc="T")
                t_bc = p.ins("tensor", lambda E_: E_.matmul(
                    pb[1][:, :], ones_row[:, :], rows[0:1, 3 * TT:4 * TT], start=True, stop=True), inc="T")
                p.wait("vector", "T", max(t_bc, xn2_free))
                v_xn2 = 0
                for d in range(DCH):
                    p.ins("vector", lambda E_, _s=sl, _d=d: E_.tensor_tensor(
                        xtm1[:, :], xpT[_s][:, _d * TT:(_d + 1) * TT], pb[0][:, :],
                        ALU.subtract), inc="V")
                    v_xn2 = p.ins("vector", lambda E_, _d=d: E_.tensor_tensor(
                        xn2[:, _d * TT:(_d + 1) * TT], xtm1[:, :], pb[1][:, :], ALU.mult), inc="V")
                pb_free[0] = ("V", v_xn2)
                pb_free[1] = ("V", v_xn2)
                p.wait("tensor", "V", v_xn2)
                p.wait_m("tensor", pb_free[2])
                t_u = 0
                for d in range(DCH):
                    t_u = p.ins("tensor", lambda E_, _d=d: E_.matmul(
                        pb[2][0:1, 0:TT], wuf_b[:, _d:_d + 1], xn2[:, _d * TT:(_d + 1) * TT],
                        start=(_d == 0), stop=(_d == DCH - 1)), inc="T")
                p.wait("scalar", "T", t_u)
                sc_u = p.ins("scalar", lambda E_: E_.activation(
                    urow[0:1, :], pb[2][0:1, 0:TT], AF.Sigmoid, bias=bu_s[0:1, 0:1]), inc="SC")
                p.wait("vector", "SC", sc_u)
                v_mr = 0
                for r in range(4):
                    p.ins("vector", lambda E_, _r=r: E_.tensor_scalar(
                        mrow[0:1, _r * TT:(_r + 1) * TT], urow[0:1, :], 4.0, float(_r),
                        ALU.mult, ALU.is_gt), inc="V")
                    p.ins("vector", lambda E_, _r=r: E_.tensor_scalar(
                        xtm1[0:1, 0:TT], urow[0:1, :], 1.0 / (_r + 1.0), None, ALU.mult), inc="V")
                    v_mr = p.ins("vector", lambda E_, _r=r: E_.tensor_tensor(
                        mrow[0:1, _r * TT:(_r + 1) * TT], mrow[0:1, _r * TT:(_r + 1) * TT],
                        xtm1[0:1, 0:TT], ALU.mult), inc="V")
                pb_free[2] = ("V", v_mr)
                p.wait("vector", "T", cb_free)
                p.wait("vector", "SW", sw_c_prev)
                v_crow = 0
                for e in range(4):
                    for r in range(4):
                        phi = (e - r) % 4
                        v_crow = p.ins("vector", lambda E_, _e=e, _r=r, _ph=phi: E_.tensor_copy(
                            bass.AP(crow, _e * TT + _ph, [[4 * TT, 1], [4, TT // 4]]),
                            bass.AP(mrow, _r * TT + _ph, [[4 * TT, 1], [4, TT // 4]])), inc="V")
                p.wait("sync", "V", v_crow)
                sw_c = p.ins("sync", lambda E_: E_.dma_start(
                    out=cscr.ap(), in_=crow[0:1, :]), inc="SW", by=16)
                sw_c_prev = sw_c
                p.wait("gpsimd", "SW", sw_c)
                p.wait("gpsimd", "T", cb_free)
                g_cbf = p.ins("gpsimd", lambda E_: E_.dma_start(
                    out=Cbf[:, :], in_=bass.AP(cscr, 0, [[TT, 4], [1, TT]])), inc="GD", by=16)
                p.wait("tensor", "V", v_crow)
                t_mb = 0
                for e in range(4):
                    p.wait_m("tensor", pb_free[e])
                    t_mb = p.ins("tensor", lambda E_, _e=e: E_.matmul(
                        pb[_e][:, :], ones_row[:, :], crow[0:1, _e * TT:(_e + 1) * TT],
                        start=True, stop=True), inc="T")
                p.wait("vector", "T", t_mb)
                v_cb = 0
                for e in range(4):
                    v_cb = p.ins("vector", lambda E_, _e=e: E_.tensor_copy(
                        Cb[:, _e * TT:(_e + 1) * TT], pb[_e][:, :]), inc="V")
                for r in range(4):
                    pb_free[r] = ("V", v_cb)

                for half in range(2):
                    es = (0, 1) if half == 0 else (2, 3)
                    for e in es:
                        for fc in range(FC):
                            k = e * FC + fc
                            wslot = k % 4
                            sw_w1 = w_load(1, e, fc, wslot)
                            bank = 4 + (k % 2)
                            p.wait("tensor", "SW", sw_w1)
                            p.wait_m("tensor", pb_free[bank])
                            t_h = 0
                            for d in range(DCH):
                                t_h = p.ins("tensor", lambda E_, _bk=bank, _w=wslot, _d=d:
                                            E_.matmul(pb[_bk][:, :],
                                                      w1b[_w][:, _d * 128:(_d + 1) * 128],
                                                      xn2[:, _d * TT:(_d + 1) * TT],
                                                      start=(_d == 0), stop=(_d == DCH - 1)),
                                            inc="T")
                            w1_free[wslot] = t_h
                            xn2_free = t_h
                            p.wait("scalar", "T", t_h)
                            p.wait("scalar", "V", hrel_free[k % 2])
                            sc_h = p.ins("scalar", lambda E_, _bk=bank, _hs=hrel[k % 2], _bc=k:
                                         E_.activation(_hs[:, :], pb[_bk][:, :], AF.Relu,
                                                       bias=b1_s[:, _bc:_bc + 1]), inc="SC")
                            pb_free[bank] = ("SC", sc_h)
                            slot = (e % 2) * FC + fc
                            p.wait("vector", "SC", sc_h)
                            v_h = p.ins("vector", lambda E_, _hs=hrel[k % 2], _sl2=slot, _e=e:
                                        E_.tensor_tensor(
                                            hsb[:, _sl2 * TT:(_sl2 + 1) * TT], _hs[:, :],
                                            Cb[:, _e * TT:(_e + 1) * TT], ALU.mult), inc="V")
                            hrel_free[k % 2] = v_h
                            hs_ready[(e, fc)] = v_h
                    t_bank_last = [0] * DCH
                    if half == 0:
                        p.wait("tensor", "GD", g_cbf)
                        for d in range(DCH):
                            p.wait_m("tensor", pb_free[d])
                            t_bank_last[d] = p.ins("tensor", lambda E_, _d=d: E_.matmul(
                                pb[_d][:, :], b2_b[:, _d * 128:(_d + 1) * 128], Cbf[:, :],
                                start=True, stop=False), inc="T")
                        cb_free = p.cnt["T"]
                    for ei, e in enumerate(es):
                        for fc in range(FC):
                            k = e * FC + fc
                            wslot = k % 4
                            sw_w2 = w_load(2, e, fc, wslot)
                            p.wait("tensor", "SW", sw_w2)
                            p.wait("tensor", "V", hs_ready[(e, fc)])
                            slot = (e % 2) * FC + fc
                            last_chunk = (ei == 1 and fc == FC - 1)
                            first_chunk = (half == 1 and ei == 0 and fc == 0)
                            for d in range(DCH):
                                if first_chunk:
                                    p.wait_m("tensor", pb_free[d])
                                t_y = p.ins("tensor",
                                            lambda E_, _d=d, _w=wslot, _sl2=slot, _st=first_chunk, _lt=last_chunk:
                                            E_.matmul(pb[_d][:, :],
                                                      w2b[_w][:, _d * 128:(_d + 1) * 128],
                                                      hsb[:, _sl2 * TT:(_sl2 + 1) * TT],
                                                      start=_st, stop=_lt), inc="T")
                                t_bank_last[d] = t_y
                            w2_free[wslot] = p.cnt["T"]
                    for d in range(DCH):
                        p.wait("vector", "T", t_bank_last[d])
                        if half == 0:
                            v_e = p.ins("vector", lambda E_, _d=d: E_.tensor_copy(
                                yac[:, _d * TT:(_d + 1) * TT], pb[_d][:, :]), inc="V")
                            pb_free[d] = ("V", v_e)
                        else:
                            p.wait("vector", "SW", outb_free[d % 2])
                            p.ins("vector", lambda E_, _d=d: E_.tensor_tensor(
                                xtm1[:, :], pb[_d][:, :], yac[:, _d * TT:(_d + 1) * TT],
                                ALU.add), inc="V")
                            v_e = p.ins("vector", lambda E_, _d=d, _ob=outb[d % 2], _s=sl:
                                        E_.tensor_tensor(
                                            _ob[:, :], xtm1[:, :], xpT[_s][:, _d * TT:(_d + 1) * TT],
                                            ALU.add), inc="V")
                            pb_free[d] = ("V", v_e)
                            p.wait("sync", "V", v_e)
                            outb_free[d % 2] = p.ins("sync",
                                                     lambda E_, _b=b, _q=qr, _d=d, _ob=outb[d % 2]:
                                                     E_.dma_start(out=out_t[_b][_q][_d], in_=_ob[:, :]),
                                                     inc="SW", by=16)
                xpT_free[sl] = p.cnt["V"]
                if ti + 2 < len(tiles):
                    nb, nq = tiles[ti + 2]
                    p.wait("gpsimd", "V", xpT_free[sl])
                    g_xp[sl] = p.ins("gpsimd", lambda E_, _b=nb, _q=nq, _s=sl:
                                     E_.dma_start(out=chunk_dst(xpT[_s]),
                                                  in_=chunk_src(xp_scr, _b, _q)),
                                     inc="GD", by=16)

            p.wait("sync", "SW", p.cnt["SW"])
            p.wait("gpsimd", "GD", p.cnt["GD"])

            with nc.Block() as block:
                for ename, dec in (("sync", block.sync), ("gpsimd", block.gpsimd),
                                   ("vector", block.vector), ("scalar", block.scalar),
                                   ("tensor", block.tensor)):
                    cmds = p.cmds[ename]
                    if not cmds:
                        continue

                    def body(Eng, _cmds=cmds):
                        for cfn in _cmds:
                            cfn(Eng)

                    dec(body)

    return nc


_CACHED = None


def _get_program():
    global _CACHED
    if _CACHED is None:
        _CACHED = build_program()
    return _CACHED


def kernel(L0_attn_x, L0_attn_Wq, L0_attn_Wk, L0_attn_Wv, L0_attn_Wu, L0_attn_rel_bias,
           L0_ffn_W1, L0_ffn_b1, L0_ffn_W2, L0_ffn_b2, L0_ffn_Wu, L0_ffn_bu):
    x = np.ascontiguousarray(np.asarray(L0_attn_x, dtype=np.float32))
    x_fm = np.ascontiguousarray(x.reshape(B, NQR, TT, DCH, 128).transpose(0, 1, 3, 4, 2))

    def wtile(W):
        W = np.asarray(W, np.float32)
        return np.ascontiguousarray(W.reshape(DCH, 128, H, 128).transpose(2, 1, 0, 3))

    wq = wtile(L0_attn_Wq); wk = wtile(L0_attn_Wk)
    wv = wtile(L0_attn_Wv); wu = wtile(L0_attn_Wu)
    rb = np.ascontiguousarray(np.asarray(L0_attn_rel_bias, np.float32).T)
    W1 = np.asarray(L0_ffn_W1, np.float32)
    w1 = np.ascontiguousarray(
        W1.reshape(E, DCH, 128, FC, 128).transpose(0, 3, 2, 1, 4).reshape(E, FC, 128, DCH * 128))
    W2 = np.asarray(L0_ffn_W2, np.float32)
    w2 = np.ascontiguousarray(W2.reshape(E, FC, 128, DCH * 128))
    b1 = np.ascontiguousarray(
        np.asarray(L0_ffn_b1, np.float32).reshape(E, FC, 128).transpose(2, 0, 1).reshape(128, E * FC))
    b2 = np.ascontiguousarray(np.asarray(L0_ffn_b2, np.float32))
    wuf = np.ascontiguousarray(np.asarray(L0_ffn_Wu, np.float32).reshape(DCH, 128).T)
    bu = np.asarray(L0_ffn_bu, np.float32).reshape(1, 1)

    nc = _get_program()
    in_maps = []
    for c in range(NCORES):
        in_maps.append(dict(
            x_fm=np.ascontiguousarray(x_fm[2 * c:2 * c + 2]),
            wq_t=wq, wk_t=wk, wv_t=wv, wu_t=wu, rb_t=rb,
            w1_t=w1, w2_t=w2, b1_t=b1, b2_t=b2, wuf_t=wuf, bu_t=bu))
    res = run_bass_kernel_spmd(nc, in_maps, core_ids=list(range(NCORES)))
    out = np.empty((B, S, D), np.float32)
    for c in range(NCORES):
        o = np.asarray(res.results[c]["out"])
        out[2 * c:2 * c + 2] = o.transpose(0, 1, 4, 2, 3).reshape(BPC, S, D)
    return out


if __name__ == "__main__":
    print("building...")
    _get_program()
    print("built ok")


# version 16
# speedup vs baseline: 1.0388x; 1.0265x over previous
import contextlib
import numpy as np
import concourse.bass as bass
import concourse.mybir as mybir
from concourse.bass_utils import run_bass_kernel_spmd

F32 = mybir.dt.float32
BF16 = mybir.dt.bfloat16
AF = mybir.ActivationFunctionType
ALU = mybir.AluOpType

NCORES = 8
B = 16; S = 2048; D = 1024; H = 8; HD = 128; FF = 4096; E = 4
BPC = 2
NQR = 4
TT = 512
DCH = 8
FC = 32
NEG = -4294967295.0
ISQ = float(np.float32(1.0 / np.sqrt(128.0)))
EPS = 1e-5
INV_D = float(np.float32(1.0 / 1024.0))

ENGS = ("tensor", "vector", "scalar", "sync", "gpsimd")


class P:

    def __init__(self):
        self.cmds = {e: [] for e in ENGS}
        self.cnt = {}
        self.handles = {}

    def sem(self, name):
        self.cnt[name] = 0

    def ins(self, eng, fn, inc=None, by=1):
        if inc is not None:
            self.cnt[inc] += by
            h = self.handles

            def run(Eng, _fn=fn, _i=inc, _b=by):
                _fn(Eng).then_inc(h[_i], _b)

            self.cmds[eng].append(run)
            return self.cnt[inc]
        self.cmds[eng].append(fn)
        return None

    def wait(self, eng, sem, val):
        if val <= 0:
            return
        assert val <= self.cnt[sem], f"wait-ahead {sem}: {val} > {self.cnt[sem]} on {eng}"
        h = self.handles
        self.cmds[eng].append(lambda Eng, _s=sem, _v=val: Eng.wait_ge(h[_s], _v))

    def wait_m(self, eng, marker):
        self.wait(eng, marker[0], marker[1])

    def barrier_all(self):
        snap = dict(self.cnt)
        for e in ENGS:
            for s, v in snap.items():
                self.wait(e, s, v)


def build_program():
    nc = bass.Bass("TRN2", target_bir_lowering=False)

    x_fm = nc.dram_tensor("x_fm", [BPC, NQR, DCH, 128, TT], F32, kind="ExternalInput")
    wq_t = nc.dram_tensor("wq_t", [H, 128, DCH, 128], F32, kind="ExternalInput")
    wk_t = nc.dram_tensor("wk_t", [H, 128, DCH, 128], F32, kind="ExternalInput")
    wv_t = nc.dram_tensor("wv_t", [H, 128, DCH, 128], F32, kind="ExternalInput")
    wu_t = nc.dram_tensor("wu_t", [H, 128, DCH, 128], F32, kind="ExternalInput")
    rb_t = nc.dram_tensor("rb_t", [H, 4095], F32, kind="ExternalInput")
    w1_t = nc.dram_tensor("w1_t", [E, FC, 128, DCH * 128], F32, kind="ExternalInput")
    w2_t = nc.dram_tensor("w2_t", [E, FC, 128, DCH * 128], F32, kind="ExternalInput")
    b1_t = nc.dram_tensor("b1_t", [128, E * FC], F32, kind="ExternalInput")
    b2_t = nc.dram_tensor("b2_t", [4, D], F32, kind="ExternalInput")
    wuf_t = nc.dram_tensor("wuf_t", [128, DCH], F32, kind="ExternalInput")
    bu_t = nc.dram_tensor("bu_t", [1, 1], F32, kind="ExternalInput")

    out_t = nc.dram_tensor("out", [BPC, NQR, DCH, 128, TT], F32, kind="ExternalOutput").ap()

    xp_scr = nc.dram_tensor("xp_scr", [BPC, NQR, DCH, 128, TT], F32, kind="Internal")
    w1bf = nc.dram_tensor("w1bf", [E, FC, 128, DCH * 128], BF16, kind="Internal")
    w2bf = nc.dram_tensor("w2bf", [E, FC, 128, DCH * 128], BF16, kind="Internal")
    kscr = nc.dram_tensor("kscr", [16, 128], BF16, kind="Internal")
    cscr = nc.dram_tensor("cscr", [4 * TT], F32, kind="Internal")

    CH_B = NQR * DCH * 128 * TT
    CH_Q = DCH * 128 * TT

    def chunk_src(tens, b, qr):
        return bass.AP(tens, b * CH_B + qr * CH_Q, [[TT, 128], [128 * TT, DCH], [1, TT]])

    def chunk_dst(sb):
        return bass.AP(sb, 0, [[DCH * TT, 128], [TT, DCH], [1, TT]])

    def wsl_src(tens, h):
        return bass.AP(tens, h * 128 * DCH * 128, [[DCH * 128, 128], [128, DCH], [1, 128]])

    def wsl_dst(sb):
        return bass.AP(sb, 0, [[DCH * 128, 128], [128, DCH], [1, 128]])

    p = P()
    for s in ("T", "V", "SC", "SW", "GD", "GC"):
        p.sem(s)

    with contextlib.ExitStack() as root:
        En = root.enter_context
        for s in ("T", "V", "SC", "SW", "GD", "GC"):
            p.handles[s] = En(nc.semaphore(s))

        ones_col = En(nc.sbuf_tensor("ones_col", [128, 1], BF16))
        ones_row = En(nc.sbuf_tensor("ones_row", [1, 128], F32))
        b1_s = En(nc.sbuf_tensor("b1_s", [128, E * FC], F32))
        b2_f = En(nc.sbuf_tensor("b2_f", [4, D], F32))
        b2_b = En(nc.sbuf_tensor("b2_b", [4, D], BF16))
        wuf_f = En(nc.sbuf_tensor("wuf_f", [128, DCH], F32))
        wuf_b = En(nc.sbuf_tensor("wuf_b", [128, DCH], BF16))
        bu_s = En(nc.sbuf_tensor("bu_s", [1, 1], F32))
        rows = En(nc.sbuf_tensor("rows", [1, 4 * TT], F32))
        eps_s = En(nc.sbuf_tensor("eps_s", [1, 1], F32))
        xtm1 = En(nc.sbuf_tensor("xtm1", [128, TT], F32))

        pb = [En(nc.psum_tensor(f"pb{i}", [128, 512], F32)) for i in range(8)]

        ident_b = En(nc.sbuf_tensor("ident_b", [128, 128], BF16))
        ones_rb = En(nc.sbuf_tensor("ones_rb", [1, TT], BF16))

        ph12 = root.enter_context(contextlib.ExitStack())
        E12 = ph12.enter_context
        xn = E12(nc.sbuf_tensor("xn", [128, DCH * BPC * S], BF16))
        negm_b = E12(nc.sbuf_tensor("negm_b", [128, 4 * TT], BF16))

        p.ins("vector", lambda E_: E_.memset(ones_col[:, :], 1.0), inc="V")
        p.ins("vector", lambda E_: E_.memset(ones_row[:, :], 1.0), inc="V")
        p.ins("vector", lambda E_: E_.memset(eps_s[:, :], EPS), inc="V")
        p.ins("vector", lambda E_: E_.memset(ones_rb[:, :], 1.0), inc="V")
        p.ins("sync", lambda E_: E_.dma_start(out=b1_s[:, :], in_=b1_t.ap()), inc="SW", by=16)
        p.ins("sync", lambda E_: E_.dma_start(out=b2_f[:, :], in_=b2_t.ap()), inc="SW", by=16)
        p.ins("sync", lambda E_: E_.dma_start(out=wuf_f[:, :], in_=wuf_t.ap()), inc="SW", by=16)
        sw0 = p.ins("sync", lambda E_: E_.dma_start(out=bu_s[:, :], in_=bu_t.ap()), inc="SW", by=16)
        p.wait("vector", "SW", sw0)
        p.ins("vector", lambda E_: E_.tensor_copy(b2_b[:, :], b2_f[:, :]), inc="V")
        p.ins("vector", lambda E_: E_.tensor_copy(wuf_b[:, :], wuf_f[:, :]), inc="V")
        for v in range(4):
            c = 128 * v - 384
            p.ins("gpsimd", lambda E_, _v=v, _c=c: E_.iota(
                negm_b[:, _v * TT:(_v + 1) * TT], [[1, TT]], base=_c,
                channel_multiplier=-1, allow_small_or_imprecise_dtypes=True), inc="GC")
        p.ins("gpsimd", lambda E_: E_.iota(
            xtm1[:, 0:128], [[1, 128]], base=0, channel_multiplier=-1,
            allow_small_or_imprecise_dtypes=True), inc="GC")
        gc_iota = p.cnt["GC"]
        p.wait("vector", "GC", gc_iota)
        p.ins("vector", lambda E_: E_.tensor_scalar(
            ident_b[:, :], xtm1[:, 0:128], 0.0, None, ALU.is_equal), inc="V")
        for v in range(4):
            sl = (v * TT, (v + 1) * TT)
            p.ins("vector", lambda E_, _s=sl: E_.tensor_scalar(
                xtm1[:, :], negm_b[:, _s[0]:_s[1]], 0.0, None, ALU.is_lt), inc="V")
            p.ins("vector", lambda E_, _s=sl: E_.tensor_scalar(
                negm_b[:, _s[0]:_s[1]], xtm1[:, :], NEG, None, ALU.mult), inc="V")

        pass0 = []
        for e in range(E):
            for k8 in range(0, FC, 8):
                pass0.append(lambda E_, _e=e, _k=k8: E_.dma_start(
                    out=w1bf.ap()[_e][_k:_k + 8], in_=w1_t.ap()[_e][_k:_k + 8]))
                pass0.append(lambda E_, _e=e, _k=k8: E_.dma_start(
                    out=w2bf.ap()[_e][_k:_k + 8], in_=w2_t.ap()[_e][_k:_k + 8]))
        gd_pass0 = 0
        for fn in pass0[:4]:
            gd_pass0 = p.ins("gpsimd", fn, inc="GD", by=16)
        pass0_rest = pass0[4:]

        with contextlib.ExitStack() as ph1:
            E1 = ph1.enter_context
            xin = [E1(nc.sbuf_tensor(f"xin{i}", [128, DCH * TT], F32)) for i in range(2)]
            sqb = E1(nc.sbuf_tensor("sqb", [128, DCH * TT], BF16))
            xbf = E1(nc.sbuf_tensor("xbf", [128, DCH * TT], BF16))

            xin_free = [0, 0]
            sq_free = 0
            xbf_free = 0
            stat_free = 0
            bc_free = 0
            g_ld = [0, 0]

            chunks = [(b, qr) for b in range(BPC) for qr in range(NQR)]
            for ci in range(2):
                b, qr = chunks[ci]
                g_ld[ci % 2] = p.ins("gpsimd", lambda E_, _b=b, _q=qr, _s=ci % 2:
                                     E_.dma_start(out=chunk_dst(xin[_s]),
                                                  in_=chunk_src(x_fm, _b, _q)),
                                     inc="GD", by=16)

            stat_free2 = [0, 0]
            bc_free2 = [0, 0]
            for ci, (b, qr) in enumerate(chunks):
                sl = ci % 2
                bb = 4 * (ci % 2)
                p.wait("scalar", "GD", g_ld[sl])
                p.wait("scalar", "T", sq_free)
                sc_sq = p.ins("scalar", lambda E_, _s=sl: E_.activation(
                    sqb[:, :], xin[_s][:, :], AF.Square), inc="SC")
                p.wait("vector", "GD", g_ld[sl])
                p.wait("vector", "T", xbf_free)
                v_cast = p.ins("vector", lambda E_, _s=sl: E_.tensor_copy(
                    xbf[:, :], xin[_s][:, :]), inc="V")
                p.wait("tensor", "SC", sc_sq)
                p.wait("tensor", "V", max(v_cast, stat_free2[ci % 2]))
                for d in range(DCH):
                    p.ins("tensor", lambda E_, _d=d, _bb=bb: E_.matmul(
                        pb[_bb][0:1, 0:TT], ones_col[:, :], xbf[:, _d * TT:(_d + 1) * TT],
                        start=(_d == 0), stop=(_d == DCH - 1)), inc="T")
                t_stats = 0
                for d in range(DCH):
                    t_stats = p.ins("tensor", lambda E_, _d=d, _bb=bb: E_.matmul(
                        pb[_bb + 1][0:1, 0:TT], ones_col[:, :], sqb[:, _d * TT:(_d + 1) * TT],
                        start=(_d == 0), stop=(_d == DCH - 1)), inc="T")
                sq_free = t_stats
                xbf_free = t_stats
                p.wait("vector", "T", t_stats)
                p.ins("vector", lambda E_, _bb=bb: E_.tensor_scalar(
                    rows[0:1, 0:TT], pb[_bb][0:1, 0:TT], INV_D, None, ALU.mult), inc="V")
                v_r = p.ins("vector", lambda E_, _bb=bb: E_.tensor_scalar(
                    rows[0:1, TT:2 * TT], pb[_bb + 1][0:1, 0:TT], INV_D, None, ALU.mult), inc="V")
                stat_free2[ci % 2] = v_r
                p.ins("vector", lambda E_: E_.tensor_tensor(
                    rows[0:1, 2 * TT:3 * TT], rows[0:1, 0:TT], rows[0:1, 0:TT], ALU.mult), inc="V")
                v_var = p.ins("vector", lambda E_: E_.tensor_tensor(
                    rows[0:1, 2 * TT:3 * TT], rows[0:1, TT:2 * TT], rows[0:1, 2 * TT:3 * TT],
                    ALU.subtract), inc="V")
                p.wait("scalar", "V", v_var)
                sc_sqrt = p.ins("scalar", lambda E_: E_.activation(
                    rows[0:1, 3 * TT:4 * TT], rows[0:1, 2 * TT:3 * TT], AF.Sqrt, bias=eps_s[0:1, 0:1]), inc="SC")
                p.wait("vector", "SC", sc_sqrt)
                v_rstd = p.ins("vector", lambda E_: E_.reciprocal(
                    rows[0:1, 3 * TT:4 * TT], rows[0:1, 3 * TT:4 * TT]), inc="V")
                p.wait("tensor", "V", max(v_rstd, bc_free2[ci % 2]))
                p.ins("tensor", lambda E_, _bb=bb: E_.matmul(
                    pb[_bb + 2][:, :], ones_row[:, :], rows[0:1, 0:TT], start=True, stop=True), inc="T")
                t_bc = p.ins("tensor", lambda E_, _bb=bb: E_.matmul(
                    pb[_bb + 3][:, :], ones_row[:, :], rows[0:1, 3 * TT:4 * TT], start=True, stop=True), inc="T")
                p.wait("vector", "T", t_bc)
                base = b * S + qr * TT
                v_xn = 0
                for d in range(DCH):
                    col = d * (BPC * S) + base
                    p.ins("vector", lambda E_, _s=sl, _d=d, _bb=bb: E_.tensor_tensor(
                        xtm1[:, :], xin[_s][:, _d * TT:(_d + 1) * TT], pb[_bb + 2][:, :],
                        ALU.subtract), inc="V")
                    v_xn = p.ins("vector", lambda E_, _c=col, _bb=bb: E_.tensor_tensor(
                        xn[:, _c:_c + TT], xtm1[:, :], pb[_bb + 3][:, :], ALU.mult), inc="V")
                bc_free2[ci % 2] = v_xn
                xin_free[sl] = v_xn
                if ci + 2 < len(chunks):
                    nb, nq = chunks[ci + 2]
                    p.wait("gpsimd", "V", xin_free[sl])
                    g_ld[sl] = p.ins("gpsimd", lambda E_, _b=nb, _q=nq, _s=sl:
                                     E_.dma_start(out=chunk_dst(xin[_s]),
                                                  in_=chunk_src(x_fm, _b, _q)),
                                     inc="GD", by=16)
            p.barrier_all()

        with contextlib.ExitStack() as ph2:
            E2 = ph2.enter_context
            strip = [E2(nc.sbuf_tensor(f"strip{i}", [128, 2432], F32)) for i in range(1)]
            stripb = [E2(nc.sbuf_tensor(f"stripb{i}", [128, 2432], BF16)) for i in range(4)]
            Bm2 = [E2(nc.sbuf_tensor(f"Bm{i}", [128, 4 * TT], BF16)) for i in range(2)]
            wb = {w: [E2(nc.sbuf_tensor(f"w{w}b{i}", [128, DCH * 128], BF16)) for i in range(2)]
                  for w in "qkvu"}
            Qb2 = [E2(nc.sbuf_tensor(f"Qb{i}", [128, S], BF16)) for i in range(2)]
            Kb2 = [E2(nc.sbuf_tensor(f"Kb{i}", [128, S], BF16)) for i in range(2)]
            Vb2 = [E2(nc.sbuf_tensor(f"Vb{i}", [128, S], BF16)) for i in range(2)]
            Us2 = [E2(nc.sbuf_tensor(f"Us{i}", [128, S], F32)) for i in range(2)]
            kab = E2(nc.sbuf_tensor("kab", [128, 4 * TT], BF16))
            wT = [E2(nc.sbuf_tensor(f"wT{i}", [128, TT], BF16)) for i in range(4)]
            nkrb2 = [E2(nc.sbuf_tensor(f"nkrb{i}", [1, S], BF16)) for i in range(2)]
            xres = [E2(nc.sbuf_tensor(f"xres{i}", [128, TT], F32)) for i in range(2)]
            xout = [E2(nc.sbuf_tensor(f"xout{i}", [128, TT], F32)) for i in range(2)]

            units = [(h, b) for h in range(H) for b in range(BPC)]
            NU = len(units)

            bm_free = [0, 0]
            nkrb_free = [0, 0]
            us_free = [0, 0]
            strip_free = [0]
            stripb_free = [0, 0, 0, 0]
            strip_ready = [0, 0, 0, 0]
            wsl_free = [0, 0]
            wsl_ready = [0, 0]
            sbank_free = [0, 0, 0, 0]
            wT_free = [0, 0, 0, 0]
            xres_free = [0, 0]
            xout_free = [0, 0]
            pb_free_v = [0] * 8
            evac_free = [("SC", 0), ("SC", 0)]
            kab_t_free = 0
            psc = {}
            vnk = {}
            vbm = {}

            def emit_strip(h, sl):
                fsl = 0
                p.wait("sync", "V", strip_free[fsl])
                last = 0
                for q in range(128):
                    off = h * 4095 + 1663 - q
                    last = p.ins("sync", lambda E_, _o=off, _q=q, _sl=fsl: E_.dma_start(
                        out=strip[_sl][_q:_q + 1, :],
                        in_=bass.AP(rb_t, _o, [[2432, 1], [1, 2432]])),
                        inc="SW", by=16)
                p.wait("vector", "SW", last)
                p.wait("vector", "T", stripb_free[sl])
                strip_free[fsl] = p.ins("vector", lambda E_, _sl=sl, _f=fsl: E_.tensor_copy(
                    stripb[_sl][:, :], strip[_f][:, :]), inc="V")
                strip_ready[sl] = strip_free[fsl]

            def emit_wslices(h, sl):
                p.wait("gpsimd", "T", wsl_free[sl])
                last = 0
                for w, src in (("q", wq_t), ("k", wk_t), ("v", wv_t), ("u", wu_t)):
                    last = p.ins("gpsimd", lambda E_, _w=w, _s=src, _h=h, _sl=sl:
                                 E_.dma_start(out=wsl_dst(wb[_w][_sl]), in_=wsl_src(_s, _h)),
                                 inc="GD", by=16)
                wsl_ready[sl] = last

            emit_strip(0, 0)
            emit_wslices(0, 0)
            emit_strip(1, 1)
            emit_wslices(1, 1)

            def emit_qr_evac(h, b, qr, par, t_last, sc_u):
                q0 = qr * TT
                abank = 2 + (qr % 2)
                er = qr % 2
                p.wait("sync", "V", xres_free[er])
                g_x = p.ins("sync", lambda E_, _b=b, _q=qr, _h=h, _e=er:
                            E_.dma_start(out=xres[_e][:, :], in_=x_fm.ap()[_b][_q][_h]),
                            inc="SW", by=16)
                p.wait("vector", "T", t_last)
                p.wait("vector", "SC", sc_u)
                p.wait("vector", "SW", g_x)
                p.wait("vector", "SW", xout_free[er])
                p.ins("vector", lambda E_, _ab=abank, _q0=q0, _p=par: E_.tensor_tensor(
                    xtm1[:, :], pb[_ab][:, :], Us2[_p][:, _q0:_q0 + TT], ALU.mult), inc="V")
                v_out = p.ins("vector", lambda E_, _e=er: E_.tensor_tensor(
                    xout[_e][:, :], xtm1[:, :], xres[_e][:, :], ALU.add), inc="V")
                pb_free_v[abank] = v_out
                xres_free[er] = v_out
                p.wait("sync", "V", v_out)
                xout_free[er] = p.ins("sync", lambda E_, _b=b, _q=qr, _h=h, _e=er:
                                      E_.dma_start(out=xp_scr.ap()[_b][_q][_h],
                                                   in_=xout[_e][:, :]),
                                      inc="SW", by=16)

            def gen_proj(u):
                nonlocal kab_t_free
                h, b = units[u]
                par = u % 2
                sl = h % 2
                ssl4 = h % 4
                xnb = b * S
                psc[u] = {}
                if b == 0:
                    if h + 2 < H:
                        emit_strip(h + 2, (h + 2) % 4)
                    p.wait("vector", "V", strip_ready[ssl4])
                    p.wait("vector", "T", bm_free[sl])
                    vb = 0
                    for v in range(4):
                        g0 = 128 * v
                        vb = p.ins("vector", lambda E_, _g=g0, _v=v, _sl=ssl4, _p=sl:
                                   E_.tensor_tensor(
                                       Bm2[_p][:, _v * TT:(_v + 1) * TT],
                                       stripb[_sl][:, _g:_g + TT],
                                       negm_b[:, _v * TT:(_v + 1) * TT], ALU.add), inc="V")
                    vbm[h] = vb
                    p.wait("tensor", "GD", wsl_ready[sl])
                yield

                def emit_proj(name, wkey, g, gi):
                    dst = {"K": Kb2, "V": Vb2, "U": Us2, "Q": Qb2}[name][par]
                    bank = pb[gi % 2]
                    p.wait_m("tensor", evac_free[gi % 2])
                    t_mm = 0
                    if name == "V":
                        for pt in range(4):
                            ptile = g * 4 + pt
                            for d in range(DCH):
                                col = d * (BPC * S) + xnb + ptile * 128
                                t_mm = p.ins("tensor",
                                             lambda E_, _bk=bank, _pt=pt, _d=d, _c=col, _w=wkey, _sl=sl:
                                             E_.matmul(_bk[:, _pt * 128:(_pt + 1) * 128],
                                                       xn[:, _c:_c + 128],
                                                       wb[_w][_sl][:, _d * 128:(_d + 1) * 128],
                                                       start=(_d == 0), stop=(_d == DCH - 1)),
                                             inc="T")
                    else:
                        for d in range(DCH):
                            col = d * (BPC * S) + xnb + g * TT
                            t_mm = p.ins("tensor",
                                         lambda E_, _bk=bank, _d=d, _c=col, _w=wkey, _sl=sl:
                                         E_.matmul(_bk[:, :],
                                                   wb[_w][_sl][:, _d * 128:(_d + 1) * 128],
                                                   xn[:, _c:_c + TT],
                                                   start=(_d == 0), stop=(_d == DCH - 1)),
                                         inc="T")
                    p.wait("scalar", "T", t_mm)
                    if name == "U":
                        p.wait("scalar", "V", us_free[par])
                    dsl = (g * TT, (g + 1) * TT)
                    sc_e = p.ins("scalar", lambda E_, _bk=bank, _d=dst, _s=dsl:
                                 E_.activation(_d[:, _s[0]:_s[1]], _bk[:, :], AF.Silu),
                                 inc="SC")
                    evac_free[gi % 2] = ("SC", sc_e)
                    psc[u][(name, g)] = sc_e

                for g in range(4):
                    emit_proj("K", "k", g, g)
                    yield
                km_t_prev = kab_t_free
                p.wait("vector", "T", km_t_prev)
                p.wait("vector", "SC", psc[u][("K", 3)])
                v_sq = 0
                for g in range(4):
                    v_sq = p.ins("vector", lambda E_, _g=g, _p=par: E_.tensor_tensor(
                        kab[:, _g * TT:(_g + 1) * TT], Kb2[_p][:, _g * TT:(_g + 1) * TT],
                        Kb2[_p][:, _g * TT:(_g + 1) * TT], ALU.mult), inc="V")
                yield
                for g in range(4):
                    emit_proj("Q", "q", g, 12 + g)
                    yield
                for g in range(4):
                    emit_proj("V", "v", g, 4 + g)
                    yield
                for g in range(4):
                    emit_proj("U", "u", g, 8 + g)
                    yield
                km_free = [0, 0]
                t_kmg = 0
                for g in range(4):
                    bk = g % 2
                    p.wait_m("tensor", evac_free[bk])
                    if g == 0:
                        p.wait("tensor", "V", v_sq)
                    p.wait("tensor", "V", km_free[bk])
                    t_kmg = p.ins("tensor", lambda E_, _g=g, _bk=bk: E_.matmul(
                        pb[_bk][0:1, 0:TT], ones_col[:, :], kab[:, _g * TT:(_g + 1) * TT],
                        start=True, stop=True), inc="T")
                    p.wait("vector", "T", t_kmg)
                    p.wait("vector", "T", nkrb_free[par])
                    p.ins("vector", lambda E_, _g=g, _bk=bk, _p=par: E_.tensor_scalar(
                        nkrb2[_p][0:1, _g * TT:(_g + 1) * TT], pb[_bk][0:1, 0:TT],
                        0.0, 1.0, ALU.is_gt, ALU.subtract), inc="V")
                    v_nk = p.ins("vector", lambda E_, _g=g, _p=par: E_.tensor_scalar(
                        nkrb2[_p][0:1, _g * TT:(_g + 1) * TT],
                        nkrb2[_p][0:1, _g * TT:(_g + 1) * TT],
                        -NEG, None, ALU.mult), inc="V")
                    km_free[bk] = v_nk
                    yield
                kab_t_free = t_kmg
                evac_free[0] = ("V", km_free[0])
                evac_free[1] = ("V", km_free[1])
                vnk[u] = max(km_free)
                wsl_free[sl] = p.cnt["T"]
                if b == 1:
                    if h + 2 < H:
                        emit_wslices(h + 2, sl)
                    for _ in range(4):
                        if pass0_rest:
                            globals_pass0()
                yield

            def globals_pass0():
                nonlocal gd_pass0
                gd_pass0 = p.ins("gpsimd", pass0_rest.pop(0), inc="GD", by=16)

            def gen_attn(u):
                h, b = units[u]
                par = u % 2
                ssl4 = h % 4
                jobs = []
                for qr in range(NQR):
                    nkq = (qr * TT) // 128 + 4
                    for kt in range(nkq):
                        jobs.append((qr, kt, nkq))
                LAG = 3
                scw = {}
                for i in range(len(jobs) + LAG):
                    if i < len(jobs):
                        qr, kt, nkq = jobs[i]
                        q0 = qr * TT
                        ssl = i % 4
                        p.wait("tensor", "SC", sbank_free[ssl])
                        p.wait("tensor", "SC", psc[u][("Q", qr)])
                        p.wait("tensor", "SC", psc[u][("K", kt // 4)])
                        g0 = q0 - kt * 128 + 384
                        if g0 < 512:
                            bsrc, boff = Bm2[h % 2], (g0 // 128) * TT
                            p.wait("tensor", "V", vbm[h])
                        else:
                            bsrc, boff = stripb[ssl4], g0
                        p.ins("tensor", lambda E_, _ssl=ssl, _kt=kt, _q0=q0, _p=par:
                              E_.matmul(pb[4 + _ssl][:, :],
                                        Kb2[_p][:, _kt * 128:(_kt + 1) * 128],
                                        Qb2[_p][:, _q0:_q0 + TT],
                                        start=True, stop=False), inc="T")
                        p.wait("tensor", "V", vnk[u])
                        p.ins("tensor", lambda E_, _ssl=ssl, _kt=kt, _p=par:
                              E_.matmul(pb[4 + _ssl][:, :],
                                        nkrb2[_p][0:1, _kt * 128:(_kt + 1) * 128],
                                        ones_rb[0:1, :],
                                        start=False, stop=False), inc="T")
                        t_sc = p.ins("tensor", lambda E_, _ssl=ssl, _bs=bsrc, _bo=boff:
                                     E_.matmul(pb[4 + _ssl][:, :], ident_b[:, :],
                                               _bs[:, _bo:_bo + TT],
                                               start=False, stop=True), inc="T")
                        p.wait("scalar", "T", t_sc)
                        p.wait("scalar", "T", wT_free[ssl])
                        sc_w = p.ins("scalar", lambda E_, _ssl=ssl: E_.activation(
                            wT[_ssl][:, :], pb[4 + _ssl][:, :], AF.Silu,
                            scale=ISQ), inc="SC")
                        sbank_free[ssl] = sc_w
                        scw[i] = sc_w
                    if i >= LAG:
                        j = i - LAG
                        qr, kt, nkq = jobs[j]
                        ssl2 = j % 4
                        abank = 2 + (qr % 2)
                        if kt == 0:
                            p.wait("tensor", "V", pb_free_v[abank])
                        p.wait("tensor", "SC", max(scw[j], psc[u][("V", kt // 4)]))
                        t_at = p.ins("tensor", lambda E_, _ssl=ssl2, _kt=kt, _ab=abank, _nk=nkq, _p=par:
                                     E_.matmul(pb[_ab][:, :],
                                               Vb2[_p][:, _kt * 128:(_kt + 1) * 128],
                                               wT[_ssl][:, :],
                                               start=(_kt == 0), stop=(_kt == _nk - 1)), inc="T")
                        wT_free[ssl2] = t_at
                        if kt == nkq - 1:
                            emit_qr_evac(h, b, qr, par, t_at, psc[u][("U", qr)])
                    yield
                stripb_free[ssl4] = p.cnt["T"]
                bm_free[h % 2] = p.cnt["T"]
                nkrb_free[par] = p.cnt["T"]
                us_free[par] = p.cnt["V"]
                yield

            gp = gen_proj(0)
            for _ in gp:
                pass
            for u in range(1, NU + 1):
                gp = gen_proj(u) if u < NU else None
                ga = gen_attn(u - 1)
                done_p = gp is None
                done_a = False
                while not (done_p and done_a):
                    if not done_p:
                        try:
                            next(gp)
                        except StopIteration:
                            done_p = True
                    if not done_a:
                        for _ in range(2):
                            try:
                                next(ga)
                            except StopIteration:
                                done_a = True
                                break
            p.barrier_all()
        ph12.close()

        with contextlib.ExitStack() as ph3:
            E3 = ph3.enter_context
            xpT = [E3(nc.sbuf_tensor(f"xpT{i}", [128, DCH * TT], F32)) for i in range(2)]
            sq3 = E3(nc.sbuf_tensor("sq3", [128, DCH * TT], BF16))
            xpb = E3(nc.sbuf_tensor("xpb", [128, DCH * TT], BF16))
            xn2 = E3(nc.sbuf_tensor("xn2", [128, DCH * TT], BF16))
            hsb = E3(nc.sbuf_tensor("hsb", [128, 64 * TT], BF16))
            yac = E3(nc.sbuf_tensor("yac", [128, DCH * TT], F32))
            Cb = E3(nc.sbuf_tensor("Cb", [128, 4 * TT], F32))
            Cbf = E3(nc.sbuf_tensor("Cbf", [4, TT], BF16))
            hrel = [E3(nc.sbuf_tensor(f"hrel{i}", [128, TT], F32)) for i in range(2)]
            w1b = [E3(nc.sbuf_tensor(f"w1b{i}", [128, DCH * 128], BF16)) for i in range(4)]
            w2b = [E3(nc.sbuf_tensor(f"w2b{i}", [128, DCH * 128], BF16)) for i in range(4)]
            outb = [E3(nc.sbuf_tensor(f"outb{i}", [128, TT], F32)) for i in range(2)]
            urow = E3(nc.sbuf_tensor("urow", [1, TT], F32))
            mrow = E3(nc.sbuf_tensor("mrow", [1, 4 * TT], F32))
            crow = E3(nc.sbuf_tensor("crow", [1, 4 * TT], F32))

            vsnap = p.cnt["V"]
            xpT_free = [vsnap, vsnap]
            g_xp = [0, 0]
            w1_free = [0, 0, 0, 0]
            w2_free = [0, 0, 0, 0]
            hrel_free = [0, 0]
            outb_free = [0, 0]
            pb_free = [("V", vsnap)] * 8
            hs_ready = {}
            xn2_free = 0
            cb_free = 0
            v_cb = 0
            sw_c_prev = 0
            sq3_free = 0
            xpb_free = 0

            tiles = [(b, qr) for b in range(BPC) for qr in range(NQR)]
            for ci in range(2):
                b, qr = tiles[ci]
                g_xp[ci % 2] = p.ins("gpsimd", lambda E_, _b=b, _q=qr, _s=ci % 2:
                                     E_.dma_start(out=chunk_dst(xpT[_s]),
                                                  in_=chunk_src(xp_scr, _b, _q)),
                                     inc="GD", by=16)

            def w_load(which, e, fc, slot):
                buf, freel, src = (w1b, w1_free, w1bf) if which == 1 else (w2b, w2_free, w2bf)
                p.wait("sync", "T", freel[slot])
                p.wait("sync", "GD", gd_pass0)
                return p.ins("sync", lambda E_, _b=buf, _s=slot, _sr=src, _e=e, _f=fc:
                             E_.dma_start(out=_b[_s][:, :], in_=_sr.ap()[_e][_f]), inc="SW", by=16)

            for ti, (b, qr) in enumerate(tiles):
                sl = ti % 2
                p.wait("scalar", "GD", g_xp[sl])
                p.wait("scalar", "T", sq3_free)
                sc_sq = p.ins("scalar", lambda E_, _s=sl: E_.activation(
                    sq3[:, :], xpT[_s][:, :], AF.Square), inc="SC")
                p.wait("vector", "GD", g_xp[sl])
                p.wait("vector", "T", xpb_free)
                v_cast = p.ins("vector", lambda E_, _s=sl: E_.tensor_copy(
                    xpb[:, :], xpT[_s][:, :]), inc="V")
                p.wait("tensor", "SC", sc_sq)
                p.wait_m("tensor", pb_free[0])
                p.wait_m("tensor", pb_free[1])
                p.wait("tensor", "V", v_cast)
                for d in range(DCH):
                    p.ins("tensor", lambda E_, _d=d: E_.matmul(
                        pb[0][0:1, 0:TT], ones_col[:, :], xpb[:, _d * TT:(_d + 1) * TT],
                        start=(_d == 0), stop=(_d == DCH - 1)), inc="T")
                t_stats = 0
                for d in range(DCH):
                    t_stats = p.ins("tensor", lambda E_, _d=d: E_.matmul(
                        pb[1][0:1, 0:TT], ones_col[:, :], sq3[:, _d * TT:(_d + 1) * TT],
                        start=(_d == 0), stop=(_d == DCH - 1)), inc="T")
                sq3_free = t_stats
                xpb_free = t_stats
                p.wait("vector", "T", t_stats)
                p.ins("vector", lambda E_: E_.tensor_scalar(
                    rows[0:1, 0:TT], pb[0][0:1, 0:TT], INV_D, None, ALU.mult), inc="V")
                p.ins("vector", lambda E_: E_.tensor_scalar(
                    rows[0:1, TT:2 * TT], pb[1][0:1, 0:TT], INV_D, None, ALU.mult), inc="V")
                p.ins("vector", lambda E_: E_.tensor_tensor(
                    rows[0:1, 2 * TT:3 * TT], rows[0:1, 0:TT], rows[0:1, 0:TT], ALU.mult), inc="V")
                v_var = p.ins("vector", lambda E_: E_.tensor_tensor(
                    rows[0:1, 2 * TT:3 * TT], rows[0:1, TT:2 * TT], rows[0:1, 2 * TT:3 * TT],
                    ALU.subtract), inc="V")
                p.wait("scalar", "V", v_var)
                sc_sqrt = p.ins("scalar", lambda E_: E_.activation(
                    rows[0:1, 3 * TT:4 * TT], rows[0:1, 2 * TT:3 * TT], AF.Sqrt, bias=eps_s[0:1, 0:1]), inc="SC")
                p.wait("vector", "SC", sc_sqrt)
                v_rstd = p.ins("vector", lambda E_: E_.reciprocal(
                    rows[0:1, 3 * TT:4 * TT], rows[0:1, 3 * TT:4 * TT]), inc="V")
                p.wait("tensor", "V", v_rstd)
                p.ins("tensor", lambda E_: E_.matmul(
                    pb[0][:, :], ones_row[:, :], rows[0:1, 0:TT], start=True, stop=True), inc="T")
                t_bc = p.ins("tensor", lambda E_: E_.matmul(
                    pb[1][:, :], ones_row[:, :], rows[0:1, 3 * TT:4 * TT], start=True, stop=True), inc="T")
                p.wait("vector", "T", max(t_bc, xn2_free))
                v_xn2 = 0
                for d in range(DCH):
                    p.ins("vector", lambda E_, _s=sl, _d=d: E_.tensor_tensor(
                        xtm1[:, :], xpT[_s][:, _d * TT:(_d + 1) * TT], pb[0][:, :],
                        ALU.subtract), inc="V")
                    v_xn2 = p.ins("vector", lambda E_, _d=d: E_.tensor_tensor(
                        xn2[:, _d * TT:(_d + 1) * TT], xtm1[:, :], pb[1][:, :], ALU.mult), inc="V")
                pb_free[0] = ("V", v_xn2)
                pb_free[1] = ("V", v_xn2)
                gate_state = {}
                def emit_gate():
                    nonlocal sw_c_prev, cb_free, v_cb
                    p.wait("tensor", "V", v_xn2)
                    p.wait_m("tensor", pb_free[2])
                    t_u = 0
                    for d in range(DCH):
                        t_u = p.ins("tensor", lambda E_, _d=d: E_.matmul(
                            pb[2][0:1, 0:TT], wuf_b[:, _d:_d + 1], xn2[:, _d * TT:(_d + 1) * TT],
                            start=(_d == 0), stop=(_d == DCH - 1)), inc="T")
                    p.wait("scalar", "T", t_u)
                    sc_u = p.ins("scalar", lambda E_: E_.activation(
                        urow[0:1, :], pb[2][0:1, 0:TT], AF.Sigmoid, bias=bu_s[0:1, 0:1]), inc="SC")
                    p.wait("vector", "SC", sc_u)
                    v_mr = 0
                    for r in range(4):
                        p.ins("vector", lambda E_, _r=r: E_.tensor_scalar(
                            mrow[0:1, _r * TT:(_r + 1) * TT], urow[0:1, :], 4.0, float(_r),
                            ALU.mult, ALU.is_gt), inc="V")
                        p.ins("vector", lambda E_, _r=r: E_.tensor_scalar(
                            xtm1[0:1, 0:TT], urow[0:1, :], 1.0 / (_r + 1.0), None, ALU.mult), inc="V")
                        v_mr = p.ins("vector", lambda E_, _r=r: E_.tensor_tensor(
                            mrow[0:1, _r * TT:(_r + 1) * TT], mrow[0:1, _r * TT:(_r + 1) * TT],
                            xtm1[0:1, 0:TT], ALU.mult), inc="V")
                    pb_free[2] = ("V", v_mr)
                    p.wait("vector", "T", cb_free)
                    p.wait("vector", "SW", sw_c_prev)
                    v_crow = 0
                    for e in range(4):
                        for r in range(4):
                            phi = (e - r) % 4
                            v_crow = p.ins("vector", lambda E_, _e=e, _r=r, _ph=phi: E_.tensor_copy(
                                bass.AP(crow, _e * TT + _ph, [[4 * TT, 1], [4, TT // 4]]),
                                bass.AP(mrow, _r * TT + _ph, [[4 * TT, 1], [4, TT // 4]])), inc="V")
                    p.wait("sync", "V", v_crow)
                    sw_c = p.ins("sync", lambda E_: E_.dma_start(
                        out=cscr.ap(), in_=crow[0:1, :]), inc="SW", by=16)
                    sw_c_prev = sw_c
                    p.wait("gpsimd", "SW", sw_c)
                    p.wait("gpsimd", "T", cb_free)
                    g_cbf = p.ins("gpsimd", lambda E_: E_.dma_start(
                        out=Cbf[:, :], in_=bass.AP(cscr, 0, [[TT, 4], [1, TT]])), inc="GD", by=16)
                    p.wait("tensor", "V", v_crow)
                    t_mb = 0
                    for e in range(4):
                        p.wait_m("tensor", pb_free[e])
                        t_mb = p.ins("tensor", lambda E_, _e=e: E_.matmul(
                            pb[_e][:, :], ones_row[:, :], crow[0:1, _e * TT:(_e + 1) * TT],
                            start=True, stop=True), inc="T")
                    p.wait("vector", "T", t_mb)
                    v_cb = 0
                    for e in range(4):
                        v_cb = p.ins("vector", lambda E_, _e=e: E_.tensor_copy(
                            Cb[:, _e * TT:(_e + 1) * TT], pb[_e][:, :]), inc="V")
                    for r in range(4):
                        pb_free[r] = ("V", v_cb)
                    gate_state['g_cbf'] = g_cbf

                for half in range(2):
                    es = (0, 1) if half == 0 else (2, 3)
                    for e in es:
                        for fc in range(FC):
                            k = e * FC + fc
                            wslot = k % 4
                            sw_w1 = w_load(1, e, fc, wslot)
                            bank = 4 + (k % 2)
                            p.wait("tensor", "SW", sw_w1)
                            p.wait_m("tensor", pb_free[bank])
                            t_h = 0
                            for d in range(DCH):
                                t_h = p.ins("tensor", lambda E_, _bk=bank, _w=wslot, _d=d:
                                            E_.matmul(pb[_bk][:, :],
                                                      w1b[_w][:, _d * 128:(_d + 1) * 128],
                                                      xn2[:, _d * TT:(_d + 1) * TT],
                                                      start=(_d == 0), stop=(_d == DCH - 1)),
                                            inc="T")
                            w1_free[wslot] = t_h
                            xn2_free = t_h
                            p.wait("scalar", "T", t_h)
                            p.wait("scalar", "V", hrel_free[k % 2])
                            sc_h = p.ins("scalar", lambda E_, _bk=bank, _hs=hrel[k % 2], _bc=k:
                                         E_.activation(_hs[:, :], pb[_bk][:, :], AF.Relu,
                                                       bias=b1_s[:, _bc:_bc + 1]), inc="SC")
                            pb_free[bank] = ("SC", sc_h)
                            if half == 0 and e == es[0] and fc == 0:
                                emit_gate()
                            slot = (e % 2) * FC + fc
                            p.wait("vector", "SC", sc_h)
                            v_h = p.ins("vector", lambda E_, _hs=hrel[k % 2], _sl2=slot, _e=e:
                                        E_.tensor_tensor(
                                            hsb[:, _sl2 * TT:(_sl2 + 1) * TT], _hs[:, :],
                                            Cb[:, _e * TT:(_e + 1) * TT], ALU.mult), inc="V")
                            hrel_free[k % 2] = v_h
                            hs_ready[(e, fc)] = v_h
                    t_bank_last = [0] * DCH
                    if half == 0:
                        p.wait("tensor", "GD", gate_state['g_cbf'])
                        for d in range(DCH):
                            p.wait_m("tensor", pb_free[d])
                            t_bank_last[d] = p.ins("tensor", lambda E_, _d=d: E_.matmul(
                                pb[_d][:, :], b2_b[:, _d * 128:(_d + 1) * 128], Cbf[:, :],
                                start=True, stop=False), inc="T")
                        cb_free = p.cnt["T"]
                    for ei, e in enumerate(es):
                        for fc in range(FC):
                            k = e * FC + fc
                            wslot = k % 4
                            sw_w2 = w_load(2, e, fc, wslot)
                            p.wait("tensor", "SW", sw_w2)
                            p.wait("tensor", "V", hs_ready[(e, fc)])
                            slot = (e % 2) * FC + fc
                            last_chunk = (ei == 1 and fc == FC - 1)
                            first_chunk = (half == 1 and ei == 0 and fc == 0)
                            for d in range(DCH):
                                if first_chunk:
                                    p.wait_m("tensor", pb_free[d])
                                t_y = p.ins("tensor",
                                            lambda E_, _d=d, _w=wslot, _sl2=slot, _st=first_chunk, _lt=last_chunk:
                                            E_.matmul(pb[_d][:, :],
                                                      w2b[_w][:, _d * 128:(_d + 1) * 128],
                                                      hsb[:, _sl2 * TT:(_sl2 + 1) * TT],
                                                      start=_st, stop=_lt), inc="T")
                                t_bank_last[d] = t_y
                            w2_free[wslot] = p.cnt["T"]
                    for d in range(DCH):
                        p.wait("vector", "T", t_bank_last[d])
                        if half == 0:
                            v_e = p.ins("vector", lambda E_, _d=d: E_.tensor_copy(
                                yac[:, _d * TT:(_d + 1) * TT], pb[_d][:, :]), inc="V")
                            pb_free[d] = ("V", v_e)
                        else:
                            p.wait("vector", "SW", outb_free[d % 2])
                            p.ins("vector", lambda E_, _d=d: E_.tensor_tensor(
                                xtm1[:, :], pb[_d][:, :], yac[:, _d * TT:(_d + 1) * TT],
                                ALU.add), inc="V")
                            v_e = p.ins("vector", lambda E_, _d=d, _ob=outb[d % 2], _s=sl:
                                        E_.tensor_tensor(
                                            _ob[:, :], xtm1[:, :], xpT[_s][:, _d * TT:(_d + 1) * TT],
                                            ALU.add), inc="V")
                            pb_free[d] = ("V", v_e)
                            p.wait("sync", "V", v_e)
                            outb_free[d % 2] = p.ins("sync",
                                                     lambda E_, _b=b, _q=qr, _d=d, _ob=outb[d % 2]:
                                                     E_.dma_start(out=out_t[_b][_q][_d], in_=_ob[:, :]),
                                                     inc="SW", by=16)
                xpT_free[sl] = p.cnt["V"]
                if ti + 2 < len(tiles):
                    nb, nq = tiles[ti + 2]
                    p.wait("gpsimd", "V", xpT_free[sl])
                    g_xp[sl] = p.ins("gpsimd", lambda E_, _b=nb, _q=nq, _s=sl:
                                     E_.dma_start(out=chunk_dst(xpT[_s]),
                                                  in_=chunk_src(xp_scr, _b, _q)),
                                     inc="GD", by=16)

            p.wait("sync", "SW", p.cnt["SW"])
            p.wait("gpsimd", "GD", p.cnt["GD"])

            with nc.Block() as block:
                for ename, dec in (("sync", block.sync), ("gpsimd", block.gpsimd),
                                   ("vector", block.vector), ("scalar", block.scalar),
                                   ("tensor", block.tensor)):
                    cmds = p.cmds[ename]
                    if not cmds:
                        continue

                    def body(Eng, _cmds=cmds):
                        for cfn in _cmds:
                            cfn(Eng)

                    dec(body)

    return nc


_CACHED = None


def _get_program():
    global _CACHED
    if _CACHED is None:
        _CACHED = build_program()
    return _CACHED


def kernel(L0_attn_x, L0_attn_Wq, L0_attn_Wk, L0_attn_Wv, L0_attn_Wu, L0_attn_rel_bias,
           L0_ffn_W1, L0_ffn_b1, L0_ffn_W2, L0_ffn_b2, L0_ffn_Wu, L0_ffn_bu):
    x = np.ascontiguousarray(np.asarray(L0_attn_x, dtype=np.float32))
    x_fm = np.ascontiguousarray(x.reshape(B, NQR, TT, DCH, 128).transpose(0, 1, 3, 4, 2))

    def wtile(W):
        W = np.asarray(W, np.float32)
        return np.ascontiguousarray(W.reshape(DCH, 128, H, 128).transpose(2, 1, 0, 3))

    wq = wtile(L0_attn_Wq); wk = wtile(L0_attn_Wk)
    wv = wtile(L0_attn_Wv); wu = wtile(L0_attn_Wu)
    rb = np.ascontiguousarray(np.asarray(L0_attn_rel_bias, np.float32).T)
    W1 = np.asarray(L0_ffn_W1, np.float32)
    w1 = np.ascontiguousarray(
        W1.reshape(E, DCH, 128, FC, 128).transpose(0, 3, 2, 1, 4).reshape(E, FC, 128, DCH * 128))
    W2 = np.asarray(L0_ffn_W2, np.float32)
    w2 = np.ascontiguousarray(W2.reshape(E, FC, 128, DCH * 128))
    b1 = np.ascontiguousarray(
        np.asarray(L0_ffn_b1, np.float32).reshape(E, FC, 128).transpose(2, 0, 1).reshape(128, E * FC))
    b2 = np.ascontiguousarray(np.asarray(L0_ffn_b2, np.float32))
    wuf = np.ascontiguousarray(np.asarray(L0_ffn_Wu, np.float32).reshape(DCH, 128).T)
    bu = np.asarray(L0_ffn_bu, np.float32).reshape(1, 1)

    nc = _get_program()
    in_maps = []
    for c in range(NCORES):
        in_maps.append(dict(
            x_fm=np.ascontiguousarray(x_fm[2 * c:2 * c + 2]),
            wq_t=wq, wk_t=wk, wv_t=wv, wu_t=wu, rb_t=rb,
            w1_t=w1, w2_t=w2, b1_t=b1, b2_t=b2, wuf_t=wuf, bu_t=bu))
    res = run_bass_kernel_spmd(nc, in_maps, core_ids=list(range(NCORES)))
    out = np.empty((B, S, D), np.float32)
    for c in range(NCORES):
        o = np.asarray(res.results[c]["out"])
        out[2 * c:2 * c + 2] = o.transpose(0, 1, 4, 2, 3).reshape(BPC, S, D)
    return out


if __name__ == "__main__":
    print("building...")
    _get_program()
    print("built ok")
